# version 13
# baseline (speedup 1.0000x reference)
import math
import numpy as np
import ml_dtypes
import concourse.bass as bass
import concourse.mybir as mybir
import concourse.bass_utils as _bu
from concourse.bass_utils import run_bass_kernel_spmd

if not getattr(_bu, "_ldw_patched", False):
    _orig_run_command = _bu.run_command

    def _run_command_ldw(cmd, *a, **k):
        cmd = [c.replace("--enable-ldw-opt=false", "--enable-ldw-opt=true")
               if isinstance(c, str) else c for c in cmd]
        return _orig_run_command(cmd, *a, **k)

    _bu.run_command = _run_command_ldw
    _bu._ldw_patched = True

F32 = mybir.dt.float32
BF16 = mybir.dt.bfloat16
AX = mybir.AxisListType.X
ALU = mybir.AluOpType
ACT = mybir.ActivationFunctionType

NCORES = 8
SEQ = 8192
D = 2048
NH = 16
HD = 128
KC = D // 128
OWN = SEQ // NCORES
HALO = 512
ROWS = OWN + HALO
TIL = ROWS // 128
TOWN = OWN // 128
CMAGIC = 12582912.0
SCALE = float(HD) ** -0.5
EPS_LN = 1e-5
EPS_RMS = 1e-6
EPS_Q = 1e-5
NEG = -1.0e30

_DEBUG = False


class Ctr:
    __slots__ = ("h", "n")

    def __init__(self, h):
        self.h = h
        self.n = 0


class Eng:

    def __init__(self, name):
        self.name = name
        self.ops = []
        self._waited = {}

    def wait(self, ctr, val=None):
        v = ctr.n if val is None else val
        if v <= 0 or self._waited.get(id(ctr), -1) >= v:
            return
        self._waited[id(ctr)] = v
        self.ops.append(lambda e, h=ctr.h, vv=v: e.wait_ge(h, vv))

    def do(self, fn, inc=None, k=1):
        if inc is None:
            self.ops.append(fn)
        else:
            inc.n += k
            self.ops.append(lambda e, f=fn, h=inc.h, kk=k: f(e).then_inc(h, kk))


def build_nc(debug=False):
    nc = bass.Bass("TRN2", target_bir_lowering=False)

    EI = dict(kind="ExternalInput")
    xh = nc.dram_tensor("xh", [ROWS, D], F32, **EI).ap()
    wqt = nc.dram_tensor("wqt", [D, D], F32, **EI).ap()
    wkt = nc.dram_tensor("wkt", [D, D], F32, **EI).ap()
    wvt = nc.dram_tensor("wvt", [D, D], F32, **EI).ap()
    wot = nc.dram_tensor("wot", [D, D], F32, **EI).ap()
    wft = nc.dram_tensor("wft", [D, D], F32, **EI).ap()
    gam = nc.dram_tensor("gam", [128, D], F32, **EI).ap()
    cosd = nc.dram_tensor("cosd", [128, TIL * 64], F32, **EI).ap()
    s2d = nc.dram_tensor("s2d", [128, TIL * 128], F32, **EI).ap()
    maskd = nc.dram_tensor("maskd", [128, 4 * 640 + 256], BF16, **EI).ap()
    identd = nc.dram_tensor("identd", [128, 128], BF16, **EI).ap()
    onesd = nc.dram_tensor("onesd", [128, 128], F32, **EI).ap()
    tcstd = nc.dram_tensor("tcstd", [128, 4], F32, **EI).ap()

    SC = dict(kind="ExternalOutput") if debug else {}
    s_wi = nc.dram_tensor("s_wi", [128, KC * D], mybir.dt.uint16, **SC).ap()
    s_x1 = nc.dram_tensor("s_x1", [OWN, D], F32, **SC).ap()
    s_x2 = nc.dram_tensor("s_x2", [OWN, D], F32, **SC).ap()
    U16 = mybir.dt.uint16
    s_k = nc.dram_tensor("s_k", [ROWS, D], U16, **SC).ap()
    s_v = nc.dram_tensor("s_v", [ROWS, D], U16, **SC).ap()
    s_q = nc.dram_tensor("s_q", [OWN, D], U16, **SC).ap()
    o_x = nc.dram_tensor("o_x", [OWN, D], F32, kind="ExternalOutput").ap()
    d_dbg = nc.dram_tensor("d_dbg", [128, 24], F32, kind="ExternalOutput").ap()

    from contextlib import ExitStack
    with ExitStack() as ctx:
        def sb(name, shape, dt):
            return ctx.enter_context(nc.sbuf_tensor(name, shape, dt))

        big0 = sb("big0", [128, KC * D], BF16)
        big1 = sb("big1", [128, TIL * D], BF16)
        gam_s = sb("gam_s", [128, D], F32)
        cos_s = sb("cos_s", [128, TIL * 64], F32)
        s2_s = sb("s2_s", [128, TIL * 128], F32)
        mask_s = sb("mask_s", [128, 4 * 640 + 256], BF16)
        ident = sb("ident", [128, 128], BF16)
        ones = sb("ones", [128, 128], F32)
        tcst = sb("tcst", [128, 4], F32)
        xbuf = [sb("xbuf0", [128, D], F32), sb("xbuf1", [128, D], F32)]
        x1b = sb("x1b", [128, D], F32)
        sA = sb("sA", [128, D], F32)
        sB = sb("sB", [128, D], F32)
        xi = [sb("xi0", [128, D], BF16), sb("xi1", [128, D], BF16)]
        xiT = sb("xiT", [128, D], BF16)
        xn2tmp = sb("xn2tmp", [128, D], BF16)
        krot = [sb("krot0", [128, D // 2], BF16), sb("krot1", [128, D // 2], BF16)]
        kfm = [sb("kfm0", [128, D // 2], BF16), sb("kfm1", [128, D // 2], BF16)]
        pbf = [sb("pbf0", [128, 640], BF16), sb("pbf1", [128, 640], BF16)]
        diagb = [sb("diagb0", [128, 128], BF16), sb("diagb1", [128, 128], BF16)]

        def tiny(name):
            return sb(name, [128, 1], F32)

        wparts = sb("wparts", [128, KC], F32)
        t_wsum = tiny("t_wsum")
        t_wtot = tiny("t_wtot")
        t_tw = tiny("t_tw")
        t_wl = tiny("t_wl")
        t_wr0 = tiny("t_wr0")
        t_wu = tiny("t_wu")
        t_wu2 = tiny("t_wu2")
        t_ws = tiny("t_ws")
        def tpair(name):
            return [tiny(name + "0"), tiny(name + "1")]

        FT = {n: tpair("f_" + n) for n in
              ("sum", "ssq", "m", "v1", "mm", "var", "arg", "ln", "inv", "b",
               "nb", "am", "t", "l2", "r0", "u", "u2", "r1", "a", "s1")}
        FT["rs"] = [tiny("f_rs0"), tiny("f_rs1"), tiny("f_rs2")]
        UT = {n: tpair("u_" + n) for n in
              ("sum", "ssq", "m", "v1", "mm", "var", "arg", "ln", "inv", "b", "nb")}
        t_s1 = tiny("t_s1")
        t_s2 = tiny("t_s2")
        C_nm = [tiny("C_nm%d" % i) for i in range(4)]
        C_rs = [tiny("C_rs%d" % i) for i in range(4)]
        C_rl = [tiny("C_rl%d" % i) for i in range(4)]
        C_ri = [tiny("C_ri%d" % i) for i in range(4)]

        psA = ctx.enter_context(nc.psum_tensor("psA", [128, D], F32))
        psB = ctx.enter_context(nc.psum_tensor("psB", [128, D], F32))

        sLx = Ctr(ctx.enter_context(nc.semaphore("sLx")))
        sLw = Ctr(ctx.enter_context(nc.semaphore("sLw")))
        sLr = Ctr(ctx.enter_context(nc.semaphore("sLr")))
        sG = Ctr(ctx.enter_context(nc.semaphore("sG")))
        sSp = Ctr(ctx.enter_context(nc.semaphore("sSp")))
        sT = Ctr(ctx.enter_context(nc.semaphore("sT")))
        sV = Ctr(ctx.enter_context(nc.semaphore("sV")))
        sC = Ctr(ctx.enter_context(nc.semaphore("sC")))

        E = {n: Eng(n) for n in ("sync", "gpsimd", "tensor", "vector", "scalar")}
        SY, GP, TE, VE, SE = (E[n] for n in ("sync", "gpsimd", "tensor", "vector", "scalar"))

        for dst, src in ((gam_s, gam), (cos_s, cosd), (s2_s, s2d),
                         (mask_s, maskd), (ident, identd), (ones, onesd),
                         (tcst, tcstd)):
            SY.do(lambda e, d=dst, s=src: e.dma_start(out=d[:, :], in_=s), sLw, 16)
        consts_ld = sLw.n

        big0f = big0[:, :].bitcast(F32)
        big1f = big1[:, :].bitcast(F32)

        def stage_ap(c):
            if c < 8:
                return big0f[:, c * D:(c + 1) * D]
            if c < 14:
                return big1f[:, (c - 8) * D:(c - 7) * D]
            return xbuf[c - 14][:, :]

        def dma_rows(e, dst, c0, c1):
            src = xh
            return None

        SY.do(lambda e: e.dma_start(
            out=big0f[:, 0:8 * D].rearrange("p (c j) -> p c j", c=8),
            in_=wft[0:1024, :].rearrange("(c p) j -> p c j", c=8)), sLw, 16)
        SY.do(lambda e: e.dma_start(
            out=big1f[:, 0:6 * D].rearrange("p (c j) -> p c j", c=6),
            in_=wft[1024:1792, :].rearrange("(c p) j -> p c j", c=6)), sLw, 16)
        SY.do(lambda e: e.dma_start(out=xbuf[0][:, :], in_=wft[1792:1920, :]), sLw, 16)
        SY.do(lambda e: e.dma_start(out=xbuf[1][:, :], in_=wft[1920:2048, :]), sLw, 16)
        wst_ld = sLw.n

        SE.wait(sLw, wst_ld)
        for c in range(KC):
            SE.do(lambda e, c=c: e.activation(
                out=xi[0][:, :], in_=stage_ap(c), func=ACT.Abs,
                accum_out=wparts[:, c:c + 1]), sC)
        VE.wait(sC, sC.n)
        VE.do(lambda e: e.reduce_sum(t_wsum[:, :], wparts[:, :], axis=AX), sV)
        TE.wait(sV, sV.n)
        TE.wait(sLw, consts_ld)
        TE.do(lambda e: e.matmul(psA[:, 0:1], ones[:, :], t_wsum[:, :],
                                 start=True, stop=True), sT)
        SE.wait(sT, sT.n)
        SE.wait(sLw, consts_ld)
        SE.do(lambda e: e.activation(out=t_wtot[:, :], in_=psA[:, 0:1], func=ACT.Identity))
        SE.do(lambda e: e.activation(out=t_tw[:, :], in_=t_wtot[:, :], func=ACT.Identity,
                                     scale=1.0 / (D * D), bias=tcst[:, 0:1]))
        SE.do(lambda e: e.activation(out=t_wl[:, :], in_=t_tw[:, :], func=ACT.Ln))
        SE.do(lambda e: e.activation(out=t_wr0[:, :], in_=t_wl[:, :],
                                     func=ACT.Exp, scale=-1.0))
        SE.do(lambda e: e.activation(out=t_wu[:, :], in_=t_tw[:, :], func=ACT.Identity,
                                     scale=t_wr0[:, :]))
        SE.do(lambda e: e.activation(out=t_wu2[:, :], in_=t_wu[:, :], func=ACT.Identity,
                                     scale=-1.0, bias=tcst[:, 2:3]))
        SE.do(lambda e: e.activation(out=t_ws[:, :], in_=t_wu2[:, :], func=ACT.Identity,
                                     scale=t_wr0[:, :]), sC)
        VE.wait(sC, sC.n)
        for c in range(KC):
            VE.do(lambda e, c=c: e.tensor_scalar(sA[:, :], stage_ap(c), t_ws[:, :],
                                                 CMAGIC, ALU.mult, ALU.add))
            VE.do(lambda e: e.tensor_scalar(sB[:, :], sA[:, :], CMAGIC, None,
                                            ALU.subtract))
            VE.do(lambda e, c=c: e.tensor_scalar(big0[:, c * D:(c + 1) * D], sB[:, :],
                                                 1.0, -1.0, ALU.min, ALU.max), sV)
        quant_done_v = sV.n
        GP.wait(sV, quant_done_v)
        GP.do(lambda e: e.dma_start(out=s_wi.bitcast(BF16), in_=big0[:, :]), sSp, 16)
        wi_spill = sSp.n
        GP.do(lambda e: e.dma_start(out=d_dbg[:, 0:16], in_=wparts[:, :]), sSp, 16)
        for ii, tb in enumerate((t_wsum, t_wtot, t_tw, t_ws, t_wr0, t_wu, t_wu2)):
            GP.do(lambda e, ii=ii, tb=tb: e.dma_start(
                out=d_dbg[:, 16 + ii:17 + ii], in_=tb[:, :],
                allow_slow_non_contiguous=True), sSp, 16)

        Ast = {}

        def ffn_prep(t, i, xsrc):
            p = t % 2
            VE.do(lambda e: e.reduce_sum(FT["sum"][p][:, :], xsrc[:, :], axis=AX), sV)
            SE.wait(sV, sV.n)
            if ("tr", t - 2) in Ast:
                SE.wait(sT, Ast[("tr", t - 2)])
            SE.do(lambda e: e.activation(out=xi[i][:, :], in_=xsrc[:, :],
                                         func=ACT.Square, accum_out=FT["ssq"][p][:, :]))
            SE.do(lambda e: e.activation(out=FT["m"][p][:, :], in_=FT["sum"][p][:, :],
                                         func=ACT.Identity, scale=1.0 / D))
            SE.do(lambda e: e.activation(out=FT["v1"][p][:, :], in_=FT["ssq"][p][:, :],
                                         func=ACT.Identity, scale=1.0 / D))
            SE.do(lambda e: e.activation(out=FT["mm"][p][:, :], in_=FT["m"][p][:, :],
                                         func=ACT.Identity, scale=FT["m"][p][:, :]))
            SE.do(lambda e: e.activation(out=FT["var"][p][:, :], in_=FT["mm"][p][:, :],
                                         func=ACT.Identity, scale=-1.0, bias=FT["v1"][p][:, :]))
            SE.do(lambda e: e.activation(out=FT["arg"][p][:, :], in_=FT["var"][p][:, :],
                                         func=ACT.Identity, scale=1.0 + EPS_RMS,
                                         bias=tcst[:, 1:2]))
            SE.do(lambda e: e.activation(out=FT["ln"][p][:, :], in_=FT["arg"][p][:, :], func=ACT.Ln))
            SE.do(lambda e: e.activation(out=FT["inv"][p][:, :], in_=FT["ln"][p][:, :],
                                         func=ACT.Exp, scale=-0.5))
            SE.do(lambda e: e.activation(out=FT["b"][p][:, :], in_=FT["m"][p][:, :],
                                         func=ACT.Identity, scale=FT["inv"][p][:, :]))
            SE.do(lambda e: e.activation(out=FT["nb"][p][:, :], in_=FT["b"][p][:, :],
                                         func=ACT.Identity, scale=-1.0), sC)
            VE.wait(sC, sC.n)
            VE.do(lambda e: e.tensor_scalar(sA[:, :], xsrc[:, :], FT["inv"][p][:, :],
                                            FT["nb"][p][:, :], ALU.mult, ALU.add))
            VE.do(lambda e: e.tensor_tensor(out=sB[:, :], in0=sA[:, :],
                                            in1=gam_s[:, :], op=ALU.mult))
            VE.do(lambda e: e.reduce_max(FT["am"][p][:, :], sB[:, :], axis=AX,
                                         apply_absolute_value=True), sV)
            SE.wait(sV, sV.n)
            SE.do(lambda e: e.activation(out=FT["t"][p][:, :], in_=FT["am"][p][:, :],
                                         func=ACT.Identity, bias=tcst[:, 0:1]))
            SE.do(lambda e: e.activation(out=FT["l2"][p][:, :], in_=FT["t"][p][:, :], func=ACT.Ln))
            SE.do(lambda e: e.activation(out=FT["r0"][p][:, :], in_=FT["l2"][p][:, :],
                                         func=ACT.Exp, scale=-1.0))
            SE.do(lambda e: e.activation(out=FT["u"][p][:, :], in_=FT["t"][p][:, :],
                                         func=ACT.Identity, scale=FT["r0"][p][:, :]))
            SE.do(lambda e: e.activation(out=FT["u2"][p][:, :], in_=FT["u"][p][:, :],
                                         func=ACT.Identity, scale=-1.0, bias=tcst[:, 2:3]))
            SE.do(lambda e: e.activation(out=FT["r1"][p][:, :], in_=FT["u2"][p][:, :],
                                         func=ACT.Identity, scale=FT["r0"][p][:, :]))
            SE.do(lambda e: e.activation(out=FT["a"][p][:, :], in_=FT["r1"][p][:, :],
                                         func=ACT.Identity, scale=127.0))
            SE.do(lambda e: e.activation(out=FT["s1"][p][:, :], in_=FT["t"][p][:, :],
                                         func=ACT.Identity, scale=t_tw[:, :]))
            SE.do(lambda e, t=t: e.activation(out=FT["rs"][t % 3][:, :], in_=FT["s1"][p][:, :],
                                         func=ACT.Identity, scale=0.5 / 127.0), sC)
            VE.wait(sC, sC.n)
            VE.do(lambda e: e.tensor_scalar(sA[:, :], sB[:, :], FT["a"][p][:, :],
                                            CMAGIC, ALU.mult, ALU.add))
            if ("tr", t - 2) in Ast:
                VE.wait(sT, Ast[("tr", t - 2)])
            VE.do(lambda e, i=i: e.tensor_scalar(xi[i][:, :], sA[:, :], CMAGIC, None,
                                                 ALU.subtract), sV)
            Ast[("quant", t)] = sV.n

        def ffnb(t):
            return (krot[0], krot[1]) if t % 2 == 0 else (kfm[0], kfm[1])

        def ffn_mm(t, i):
            p = t % 2
            TE.wait(sV, Ast[("quant", t)])
            if ("evacxn2", t - 2) in Ast:
                TE.wait(sC, Ast[("evacxn2", t - 2)])
            for c in range(KC):
                TE.do(lambda e, c=c, i=i: e.matmul(
                    psA[:, c * 128:(c + 1) * 128], xi[i][:, c * 128:(c + 1) * 128],
                    ident[:, :], start=True, stop=True), sT if c == KC - 1 else None)
            Ast[("tr", t)] = sT.n
            SE.wait(sT, sT.n)
            SE.do(lambda e: e.activation(out=xiT[:, :], in_=psA[:, :], func=ACT.Copy), sC)
            Ast[("evacxiT", t)] = sC.n
            TE.wait(sC, sC.n)
            if ("evacffn", t - 1) in Ast:
                TE.wait(sC, Ast[("evacffn", t - 1)])
            for c in range(KC):
                for ss in range(4):
                    TE.do(lambda e, c=c, ss=ss: e.matmul(
                        psB[:, ss * 512:(ss + 1) * 512],
                        xiT[:, c * 128:(c + 1) * 128],
                        big0[:, c * D + ss * 512: c * D + (ss + 1) * 512],
                        start=(c == 0), stop=(c == KC - 1)),
                        sT if (c == KC - 1 and ss == 3) else None)
            Ast[("mm", t)] = sT.n

        def emit_evacffn(t):
            p = t % 2
            fb = ffnb(t)
            SE.wait(sT, Ast[("mm", t)])
            if ("x1", t - 2) in Ast:
                SE.wait(sV, Ast[("x1", t - 2)])
            SE.do(lambda e, fb=fb, t=t: e.activation(
                out=fb[0][:, :], in_=psB[:, 0:1024], func=ACT.Identity,
                scale=FT["rs"][t % 3][:, :]))
            SE.do(lambda e, fb=fb, t=t: e.activation(
                out=fb[1][:, :], in_=psB[:, 1024:2048], func=ACT.Identity,
                scale=FT["rs"][t % 3][:, :]), sC)
            Ast[("evacffn", t)] = sC.n

        def emit_resid(t, xsrc, xdst):
            fb = ffnb(t)
            VE.wait(sC, Ast[("evacffn", t)])
            VE.do(lambda e, fb=fb, xsrc=xsrc, xdst=xdst: e.tensor_tensor(
                out=xdst[:, 0:1024], in0=xsrc[:, 0:1024], in1=fb[0][:, :], op=ALU.add))
            VE.do(lambda e, fb=fb, xsrc=xsrc, xdst=xdst: e.tensor_tensor(
                out=xdst[:, 1024:2048], in0=xsrc[:, 1024:2048], in1=fb[1][:, :],
                op=ALU.add), sV)
            Ast[("x1", t)] = sV.n

        x1_spills = {}

        def xslot(t):
            return (xbuf[0], xbuf[1], x1b)[t % 3]

        def emit_xn2tr(t):
            TE.wait(sV, Ast[("xn2", t)])
            key = ("evacxiT", t + 2) if ("evacxiT", t + 2) in Ast else (
                ("evacxiT", t + 1) if ("evacxiT", t + 1) in Ast else ("evacxiT", t))
            TE.wait(sC, Ast[key])
            for c in range(KC):
                TE.do(lambda e, c=c: e.matmul(
                    psA[:, c * 128:(c + 1) * 128], xn2tmp[:, c * 128:(c + 1) * 128],
                    ident[:, :], start=True, stop=True), sT if c == KC - 1 else None)
            Ast[("xn2tr", t)] = sT.n
            SE.wait(sT, sT.n)
            SE.do(lambda e, t=t: e.activation(out=big1[:, t * D:(t + 1) * D],
                                              in_=psA[:, :], func=ACT.Copy), sC)
            Ast[("evacxn2", t)] = sC.n

        def emit_A_tail(t):
            p = t % 2
            xs = xslot(t)
            emit_resid(t, xs, xs)
            if t >= 4:
                GP.wait(sV, Ast[("x1", t)])
                GP.do(lambda e, t=t, xs=xs: e.dma_start(
                    out=s_x1[(t - 4) * 128:(t - 3) * 128, :], in_=xs[:, :]), sSp, 16)
                x1_spills[t] = sSp.n
            VE.do(lambda e, xs=xs: e.reduce_sum(UT["sum"][p][:, :], xs[:, :],
                                                axis=AX), sV)
            SE.wait(sV, sV.n)
            if ("xn2tr", t - 1) in Ast:
                SE.wait(sT, Ast[("xn2tr", t - 1)])
            SE.do(lambda e, xs=xs: e.activation(out=xn2tmp[:, :], in_=xs[:, :],
                                                func=ACT.Square, accum_out=UT["ssq"][p][:, :]))
            SE.do(lambda e: e.activation(out=UT["m"][p][:, :], in_=UT["sum"][p][:, :],
                                         func=ACT.Identity, scale=1.0 / D))
            SE.do(lambda e: e.activation(out=UT["v1"][p][:, :], in_=UT["ssq"][p][:, :],
                                         func=ACT.Identity, scale=1.0 / D))
            SE.do(lambda e: e.activation(out=UT["mm"][p][:, :], in_=UT["m"][p][:, :],
                                         func=ACT.Identity, scale=UT["m"][p][:, :]))
            SE.do(lambda e: e.activation(out=UT["var"][p][:, :], in_=UT["mm"][p][:, :],
                                         func=ACT.Identity, scale=-1.0, bias=UT["v1"][p][:, :]))
            SE.do(lambda e: e.activation(out=UT["arg"][p][:, :], in_=UT["var"][p][:, :],
                                         func=ACT.Identity, bias=tcst[:, 3:4]))
            SE.do(lambda e: e.activation(out=UT["ln"][p][:, :], in_=UT["arg"][p][:, :], func=ACT.Ln))
            SE.do(lambda e: e.activation(out=UT["inv"][p][:, :], in_=UT["ln"][p][:, :],
                                         func=ACT.Exp, scale=-0.5))
            SE.do(lambda e: e.activation(out=UT["b"][p][:, :], in_=UT["m"][p][:, :],
                                         func=ACT.Identity, scale=UT["inv"][p][:, :]))
            SE.do(lambda e: e.activation(out=UT["nb"][p][:, :], in_=UT["b"][p][:, :],
                                         func=ACT.Identity, scale=-1.0), sC)
            VE.wait(sC, sC.n)
            if ("xn2tr", t - 1) in Ast:
                VE.wait(sT, Ast[("xn2tr", t - 1)])
            VE.do(lambda e, xs=xs: e.tensor_scalar(xn2tmp[:, :], xs[:, :],
                                                   UT["inv"][p][:, :], UT["nb"][p][:, :],
                                                   ALU.mult, ALU.add), sV)
            Ast[("xn2", t)] = sV.n

        SY.wait(sV, quant_done_v)
        SY.do(lambda e: e.dma_start(out=xbuf[0][:, :], in_=xh[0:128, :]), sLx, 16)
        Ast[("xld", 0)] = sLx.n
        for t in range(TIL):
            i = t % 2
            VE.wait(sLx, Ast[("xld", t)])
            SE.wait(sLx, Ast[("xld", t)])
            ffn_prep(t, i, xslot(t))
            if t >= 2:
                emit_evacffn(t - 2)
            if t >= 1:
                ffn_mm(t - 1, (t - 1) % 2)
            if t >= 2:
                emit_A_tail(t - 2)
                emit_xn2tr(t - 2)
            if t + 1 < TIL:
                if t + 1 <= 2:
                    SY.wait(sV, quant_done_v)
                if ("x1", t - 2) in Ast:
                    SY.wait(sV, Ast[("xn2", t - 2)])
                    if t - 2 >= 4:
                        SY.wait(sSp, x1_spills[t - 2])
                SY.do(lambda e, t1=t + 1: e.dma_start(
                    out=xslot(t1)[:, :], in_=xh[t1 * 128:(t1 + 1) * 128, :]), sLx, 16)
                Ast[("xld", t + 1)] = sLx.n
        emit_evacffn(TIL - 2)
        ffn_mm(TIL - 1, (TIL - 1) % 2)
        emit_A_tail(TIL - 2)
        emit_xn2tr(TIL - 2)
        emit_evacffn(TIL - 1)
        emit_A_tail(TIL - 1)
        emit_xn2tr(TIL - 1)

        HW = KC * D // 2
        subs = []
        for wsrc, kind in ((wkt, "k"), (wvt, "v"), (wqt, "q")):
            for hf in range(2):
                subs.append((wsrc, kind, hf))
        sub_done_T = [0] * len(subs)
        TE.wait(sC, sC.n)

        def emit_rope_or_evac(kind, t, tp, hf):
            po = tp * 1024
            if kind == "v":
                SE.wait(sT, Bst["mm"][t])
                if t - 2 in Bst["spill"]:
                    SE.wait(sSp, Bst["spill"][t - 2])
                SE.do(lambda e, tp=tp: e.activation(
                    out=krot[tp][:, :], in_=psA[:, tp * 1024:(tp + 1) * 1024],
                    func=ACT.Copy), sC)
                Bst["rope"][t] = ("C", sC.n)
            else:
                VE.wait(sT, Bst["mm"][t])
                if t - 2 in Bst["tr"]:
                    VE.wait(sT, Bst["tr"][t - 2])
                c2 = bass.AP(cos_s, t * 64,
                             [[TIL * 64, 128], [0, 8], [1, 64], [0, 2]])
                s2 = bass.AP(s2_s, t * 128,
                             [[TIL * 128, 128], [0, 8], [1, 128]])
                qv = bass.AP(psA, po, [[D, 128], [128, 8], [2, 64], [1, 2]])
                qsw = bass.AP(psA, po + 1, [[D, 128], [128, 8], [2, 64], [-1, 2]])
                sAh = bass.AP(sA, 0, [[D, 128], [128, 8], [2, 64], [1, 2]])
                sBh = bass.AP(sB, 0, [[D, 128], [128, 8], [2, 64], [1, 2]])
                VE.do(lambda e, qv=qv, c2=c2, sAh=sAh: e.tensor_tensor(
                    out=sAh, in0=qv, in1=c2, op=ALU.mult))
                VE.do(lambda e, qsw=qsw, s2=s2, sBh=sBh: e.tensor_tensor(
                    out=sBh, in0=qsw, in1=s2, op=ALU.mult))
                VE.do(lambda e, tp=tp: e.tensor_tensor(
                    out=krot[tp][:, :], in0=sA[:, 0:1024], in1=sB[:, 0:1024],
                    op=ALU.add), sV)
                Bst["rope"][t] = ("V", sV.n)

        def emit_tr_evac_spill(kind, t, tp, hf):
            if kind == "v":
                eng, val = Bst["rope"][t]
                GP.wait(sC, val)
                GP.do(lambda e, t=t, hf=hf, tp=tp: e.dma_start(
                    out=s_v[t * 128:(t + 1) * 128,
                            hf * 1024:(hf + 1) * 1024].bitcast(BF16),
                    in_=krot[tp][:, :]), sSp, 16)
                Bst["spill"][t] = sSp.n
                return
            eng, val = Bst["rope"][t]
            TE.wait(sV, val)
            if t - 1 in Bst["evac"]:
                TE.wait(sC, Bst["evac"][t - 1])
            for hh in range(8):
                TE.do(lambda e, hh=hh, tp=tp: e.matmul(
                    psB[:, hh * 128:(hh + 1) * 128],
                    krot[tp][:, hh * 128:(hh + 1) * 128],
                    ident[:, :], start=True, stop=True),
                    sT if hh == 7 else None)
            Bst["tr"][t] = sT.n
            SE.wait(sT, Bst["tr"][t])
            if t - 2 in Bst["spill"]:
                SE.wait(sSp, Bst["spill"][t - 2])
            kscale = SCALE if kind == "k" else 1.0
            SE.do(lambda e, tp=tp, ks=kscale: e.activation(
                out=kfm[tp][:, :], in_=psB[:, 0:1024], func=ACT.Identity,
                scale=ks), sC)
            Bst["evac"][t] = sC.n
            GP.wait(sC, Bst["evac"][t])
            dst = s_k if kind == "k" else s_q
            toff = t if kind == "k" else t - 4
            GP.do(lambda e, dst=dst, toff=toff, hf=hf, tp=tp: e.dma_start(
                out=dst[toff * 128:(toff + 1) * 128,
                        hf * 1024:(hf + 1) * 1024].bitcast(BF16),
                in_=kfm[tp][:, :]), sSp, 16)
            Bst["spill"][t] = sSp.n

        def emit_wload(si):
            wsrc, kind, hf = subs[si]
            half = si % 2
            if si < 2:
                GP.wait(sT, sT.n)
            else:
                GP.wait(sT, sub_done_T[si - 2])
            GP.do(lambda e, w=wsrc, hf=hf, half=half: e.dma_start(
                out=big0[:, half * HW:(half + 1) * HW].rearrange("p (c j) -> p c j", c=KC),
                in_=w[:, hf * (D // 2):(hf + 1) * (D // 2)]
                    .rearrange("(c p) j -> p c j", c=KC)), sG, 16)
            return sG.n

        wload_done = [0] * len(subs)
        wload_done[0] = emit_wload(0)
        wload_done[1] = emit_wload(1)
        for si, (wsrc, kind, hf) in enumerate(subs):
            half = si % 2
            Bst = {"mm": {}, "rope": {}, "tr": {}, "evac": {}, "spill": {}}
            tiles = list(range(TIL)) if kind in ("k", "v") else list(range(4, TIL))
            TE.wait(sG, wload_done[si])
            TE.wait(sV, sV.n)
            TE.wait(sC, sC.n)
            VE.wait(sSp, sSp.n)
            SE.wait(sSp, sSp.n)
            prev = None
            for t in tiles:
                tp = t % 2
                if t - 2 in Bst["rope"]:
                    eng, val = Bst["rope"][t - 2]
                    if eng == "V":
                        TE.wait(sV, val)
                    else:
                        TE.wait(sC, val)
                for c in range(KC):
                    for ss in range(2):
                        TE.do(lambda e, c=c, ss=ss, t=t, half=half, tp=tp: e.matmul(
                            psA[:, tp * 1024 + ss * 512: tp * 1024 + (ss + 1) * 512],
                            big1[:, t * D + c * 128:t * D + (c + 1) * 128],
                            big0[:, half * HW + c * 1024 + ss * 512:
                                    half * HW + c * 1024 + (ss + 1) * 512],
                            start=(c == 0), stop=(c == KC - 1)),
                            sT if (c == KC - 1 and ss == 1) else None)
                Bst["mm"][t] = sT.n
                emit_rope_or_evac(kind, t, tp, hf)
                if prev is not None:
                    emit_tr_evac_spill(kind, prev, prev % 2, hf)
                prev = t
            emit_tr_evac_spill(kind, prev, prev % 2, hf)
            sub_done_T[si] = sT.n
            if si + 2 < len(subs):
                wload_done[si + 2] = emit_wload(si + 2)

        qsl = sA[:, :].bitcast(BF16)

        def kslot(bt):
            return big1[:, (bt % 6) * D:((bt % 6) + 1) * D]

        def vslot(bt):
            return big1[:, (6 + bt % 6) * D:(7 + bt % 6) * D]

        def qslot(qt):
            return qsl[:, (qt % 2) * D:((qt % 2) + 1) * D]

        def x1slot(qt):
            return (x1b if qt % 2 == 0 else sB)

        attnfm = xi[0]
        x2b = xbuf[0]

        GP.wait(sT, sT.n)
        GP.do(lambda e: e.dma_start(
            out=big0[:, :].rearrange("p (c j) -> p c j", c=KC),
            in_=wot.rearrange("(c p) j -> p c j", c=KC)), sG, 16)
        wo_ld = sG.n

        SY.wait(sT, sT.n)
        SY.wait(sV, sV.n)
        SY.wait(sSp, sSp.n)
        for bt in range(6):
            SY.do(lambda e, bt=bt: e.dma_start(
                out=kslot(bt), in_=s_k[bt * 128:(bt + 1) * 128, :].bitcast(BF16)), sLr, 16)
            SY.do(lambda e, bt=bt: e.dma_start(
                out=vslot(bt), in_=s_v[bt * 128:(bt + 1) * 128, :].bitcast(BF16)), sLr, 16)
        for q0 in (0, 1):
            SY.do(lambda e, q0=q0: e.dma_start(
                out=qslot(q0), in_=s_q[q0 * 128:(q0 + 1) * 128, :].bitcast(BF16)), sLr, 16)
        x1_ld = {}
        for q0 in (0, 1):
            SY.wait(sSp, x1_spills[q0 + 4])
            SY.do(lambda e, q0=q0: e.dma_start(
                out=x1slot(q0)[:, :], in_=s_x1[q0 * 128:(q0 + 1) * 128, :]), sLx, 16)
            x1_ld[q0] = sLx.n
        ring_ready = {0: sLr.n}

        NSUP = NH // 2
        SLOT = [(psB, 0), (psA, 512)]

        def sblk(par, j, r=None, n=640):
            tns, base = SLOT[par]
            off = base + j * 640 + (0 if r is None else r * 128)
            ln = n if r is None else 128
            return tns[:, off:off + ln]

        def s2h_ap(par, off, blocklen):
            tns, base = SLOT[par]
            return bass.AP(tns, base + off, [[2048, 128], [640, 2], [1, blocklen]])

        psO = psA[:, 0:256]
        psW = psA[:, 1024:2048]
        pbfset = [(pbf[0], pbf[1]), (kfm[0][:, 0:640], kfm[1][:, 0:640])]
        diagset = [(diagb[0], krot[0][:, 0:128]), (diagb[1], krot[1][:, 0:128])]

        evacPT_done = [0, 0]
        sc_done = {}
        vmax_done = {}
        recip_done = {}
        diag_done = {}
        pt_done = {}
        av_done = {}
        x2_spill = {}
        sc_last = {}
        x2evac_done = {}
        evacOh_done = {}
        ring_rd = {0: ring_ready[0]}
        psO_c = [psA[:, 0:256], psA[:, 1792:2048]]
        psWq = psB[:, 1536:2048]
        attnfm_c = [xi[0], xi[1]]

        def c_scores(qt, m, ch):
            g = qt * NSUP + m
            h0 = 2 * m
            TE.wait(sLr, ring_rd[qt // 2])
            if evacPT_done[ch]:
                TE.wait(sV, evacPT_done[ch])
            if ch == 1:
                pg0 = g - 2 if m >= 2 else (qt - 2) * NSUP + NSUP - 2 + m
                if pg0 in evacOh_done:
                    TE.wait(sC, evacOh_done[pg0])
            for j in range(2):
                for r in range(5):
                    TE.do(lambda e, h=h0 + j, r=r, qt=qt, ch=ch, j=j: e.matmul(
                        sblk(ch, j, r),
                        qslot(qt)[:, h * 128:(h + 1) * 128],
                        kslot(qt + r)[:, h * 128:(h + 1) * 128],
                        start=True, stop=True),
                        sT if (j == 1 and r == 4) else None)
            sc_done[g] = sT.n
            if m == NSUP - 1:
                sc_last[qt] = sT.n
            VE.wait(sT, sc_done[g])
            if qt < 4:
                VE.do(lambda e, ch=ch, qt=qt: e.tensor_tensor(
                    out=s2h_ap(ch, 0, 640), in0=s2h_ap(ch, 0, 640),
                    in1=bass.AP(mask_s, qt * 640,
                                [[2816, 128], [0, 2], [1, 640]]),
                    op=ALU.add))
            else:
                VE.do(lambda e, ch=ch: e.tensor_tensor(
                    out=s2h_ap(ch, 0, 128), in0=s2h_ap(ch, 0, 128),
                    in1=bass.AP(mask_s, 2560, [[2816, 128], [0, 2], [1, 128]]),
                    op=ALU.add))
                VE.do(lambda e, ch=ch: e.tensor_tensor(
                    out=s2h_ap(ch, 512, 128), in0=s2h_ap(ch, 512, 128),
                    in1=bass.AP(mask_s, 2688, [[2816, 128], [0, 2], [1, 128]]),
                    op=ALU.add))
            for j in range(2):
                VE.do(lambda e, ch=ch, j=j: e.reduce_max(
                    C_nm[ch * 2 + j][:, :], sblk(ch, j), axis=AX,
                    negate=True), sV if j == 1 else None)
            vmax_done[g] = sV.n
            SE.wait(sV, vmax_done[g])
            pg = g - 1 if m > 0 else (qt - 2) * NSUP + NSUP - 1
            if pg in av_done:
                SE.wait(sT, av_done[pg])
            for j in range(2):
                SE.do(lambda e, ch=ch, j=j: e.activation(
                    out=pbfset[ch][j][:, :], in_=sblk(ch, j), func=ACT.Exp,
                    bias=C_nm[ch * 2 + j][:, :], scale=1.0,
                    accum_out=C_rs[ch * 2 + j][:, :]))
            for j in range(2):
                SE.do(lambda e, ch=ch, j=j: e.activation(
                    out=C_rl[ch * 2 + j][:, :], in_=C_rs[ch * 2 + j][:, :],
                    func=ACT.Ln))
                SE.do(lambda e, ch=ch, j=j: e.activation(
                    out=C_ri[ch * 2 + j][:, :], in_=C_rl[ch * 2 + j][:, :],
                    func=ACT.Exp, scale=-1.0),
                    sC if j == 1 else None)
            recip_done[g] = sC.n

        def c_ptav(qt, m, ch):
            g = qt * NSUP + m
            VE.wait(sC, recip_done[g])
            pg = g - 1 if m > 0 else (qt - 2) * NSUP + NSUP - 1
            if pg in pt_done:
                VE.wait(sT, pt_done[pg])
            for j in range(2):
                VE.do(lambda e, ch=ch, j=j: e.tensor_scalar(
                    diagset[ch][j][:, :], ident[:, :],
                    C_ri[ch * 2 + j][:, :], None, ALU.mult),
                    sV if j == 1 else None)
            diag_done[g] = sV.n
            TE.wait(sV, diag_done[g])
            if ch == 1:
                pgO0 = g - 1 if m > 0 else (qt - 2) * NSUP + NSUP - 1
                if pgO0 in evacOh_done:
                    TE.wait(sC, evacOh_done[pgO0])
            for j in range(2):
                for r in range(5):
                    TE.do(lambda e, r=r, ch=ch, j=j: e.matmul(
                        sblk(ch, j, r),
                        pbfset[ch][j][:, r * 128:(r + 1) * 128],
                        diagset[ch][j][:, :], start=True, stop=True),
                        sT if (j == 1 and r == 4) else None)
            pt_done[g] = sT.n
            VE.wait(sT, pt_done[g])
            for j in range(2):
                VE.do(lambda e, ch=ch, j=j: e.tensor_copy(
                    out=pbfset[ch][j][:, :], in_=sblk(ch, j)),
                    sV if j == 1 else None)
            evacPT_done[ch] = sV.n
            pgO = g - 1 if m > 0 else (qt - 2) * NSUP + NSUP - 1
            if pgO in evacOh_done:
                TE.wait(sC, evacOh_done[pgO])
            TE.wait(sV, evacPT_done[ch])
            for j in range(2):
                for r in range(5):
                    TE.do(lambda e, h=2 * m + j, r=r, qt=qt, ch=ch, j=j: e.matmul(
                        psO_c[ch][:, j * 128:(j + 1) * 128],
                        vslot(qt + r)[:, h * 128:(h + 1) * 128],
                        pbfset[ch][j][:, r * 128:(r + 1) * 128],
                        start=(r == 0), stop=(r == 4)),
                        sT if (j == 1 and r == 4) else None)
            av_done[g] = sT.n

        def c_evacOh(qt, m, ch):
            g = qt * NSUP + m
            SE.wait(sT, av_done[g])
            SE.do(lambda e, m=m, ch=ch: e.activation(
                out=attnfm_c[ch][:, m * 256:(m + 1) * 256], in_=psO_c[ch],
                func=ACT.Copy), sC)
            evacOh_done[g] = sC.n

        def c_wo(qt, ch):
            TE.wait(sC, evacOh_done[qt * NSUP + NSUP - 1])
            TE.wait(sG, wo_ld)
            VE.wait(sLx, x1_ld[qt])
            if qt > 0:
                VE.wait(sSp, x2_spill[qt - 1])
            for jq in range(4):
                if qt > 0 or jq > 0:
                    TE.wait(sV, sV.n)
                for cc in range(KC):
                    TE.do(lambda e, cc=cc, jq=jq, ch=ch: e.matmul(
                        psWq,
                        attnfm_c[ch][:, cc * 128:(cc + 1) * 128],
                        big0[:, cc * D + jq * 512:cc * D + (jq + 1) * 512],
                        start=(cc == 0), stop=(cc == KC - 1)),
                        sT if cc == KC - 1 else None)
                VE.wait(sT, sT.n)
                VE.do(lambda e, jq=jq, qt=qt: e.tensor_tensor(
                    out=x2b[:, jq * 512:(jq + 1) * 512],
                    in0=x1slot(qt)[:, jq * 512:(jq + 1) * 512],
                    in1=psWq, op=ALU.add), sV)
            GP.wait(sV, sV.n)
            GP.do(lambda e, qt=qt: e.dma_start(
                out=s_x2[qt * 128:(qt + 1) * 128, :], in_=x2b[:, :]), sSp, 16)
            x2_spill[qt] = sSp.n
            x2evac_done[qt] = sV.n

        for qp in range(TOWN // 2):
            qA, qB = 2 * qp, 2 * qp + 1
            for k in range(NSUP + 2):
                for ch, qt in ((0, qA), (1, qB)):
                    if 0 <= k - 2 < NSUP:
                        c_evacOh(qt, k - 2, ch)
                    if 0 <= k - 1 < NSUP:
                        c_ptav(qt, k - 1, ch)
                        if k - 1 == NSUP - 1 and qp + 1 < TOWN // 2:
                            bt = qt + 6
                            SY.wait(sT, av_done[qt * NSUP + NSUP - 1])
                            SY.do(lambda e, bt=bt: e.dma_start(
                                out=vslot(bt),
                                in_=s_v[bt * 128:(bt + 1) * 128, :].bitcast(BF16)),
                                sLr, 16)
                    if k < NSUP:
                        c_scores(qt, k, ch)
                        if k == NSUP - 1 and qp + 1 < TOWN // 2:
                            bt = qt + 6
                            SY.wait(sT, sc_last[qt])
                            SY.do(lambda e, bt=bt: e.dma_start(
                                out=kslot(bt),
                                in_=s_k[bt * 128:(bt + 1) * 128, :].bitcast(BF16)),
                                sLr, 16)
                            SY.do(lambda e, qn=qt + 2: e.dma_start(
                                out=qslot(qn),
                                in_=s_q[qn * 128:(qn + 1) * 128, :].bitcast(BF16)),
                                sLr, 16)
            c_wo(qA, 0)
            c_wo(qB, 1)
            if qp + 1 < TOWN // 2:
                for qn in (qA + 2, qB + 2):
                    if qn - 2 in x2evac_done:
                        SY.wait(sV, x2evac_done[qn - 2])
                    SY.wait(sSp, x1_spills[qn + 4])
                    SY.do(lambda e, qn=qn: e.dma_start(
                        out=x1slot(qn)[:, :], in_=s_x1[qn * 128:(qn + 1) * 128, :]), sLx, 16)
                    x1_ld[qn] = sLx.n
                ring_rd[qp + 1] = sLr.n

        SY.wait(sT, sT.n)
        VE.wait(sT, sT.n)
        SE.wait(sT, sT.n)
        SY.wait(sSp, wi_spill)
        SY.do(lambda e: e.dma_start(out=big0[:, :], in_=s_wi.bitcast(BF16)), sLw, 16)
        wi_reld = sLw.n
        TE.wait(sLw, wi_reld)
        out_stores = {}
        Ast.clear()

        def emit_D_tail(t):
            xs = xslot(t)
            emit_resid(t, xs, xs)
            GP.wait(sV, Ast[("x1", t)])
            GP.do(lambda e, t=t, xs=xs: e.dma_start(
                out=o_x[t * 128:(t + 1) * 128, :], in_=xs[:, :]), sSp, 16)
            out_stores[t] = sSp.n

        SY.wait(sV, sV.n)
        SY.wait(sSp, x2_spill[0])
        SY.do(lambda e: e.dma_start(out=xbuf[0][:, :], in_=s_x2[0:128, :]), sLx, 16)
        Ast[("xld", 0)] = sLx.n
        for t in range(TOWN):
            i = t % 2
            VE.wait(sLx, Ast[("xld", t)])
            SE.wait(sLx, Ast[("xld", t)])
            ffn_prep(t, i, xslot(t))
            if t >= 2:
                emit_evacffn(t - 2)
            if t >= 1:
                ffn_mm(t - 1, (t - 1) % 2)
            if t >= 2:
                emit_D_tail(t - 2)
            if t + 1 < TOWN:
                if t + 1 <= 2:
                    SY.wait(sV, sV.n)
                if t - 2 in out_stores:
                    SY.wait(sSp, out_stores[t - 2])
                SY.wait(sSp, x2_spill[t + 1])
                SY.do(lambda e, t1=t + 1: e.dma_start(
                    out=xslot(t1)[:, :], in_=s_x2[t1 * 128:(t1 + 1) * 128, :]), sLx, 16)
                Ast[("xld", t + 1)] = sLx.n
        emit_evacffn(TOWN - 2)
        ffn_mm(TOWN - 1, (TOWN - 1) % 2)
        emit_D_tail(TOWN - 2)
        emit_evacffn(TOWN - 1)
        emit_D_tail(TOWN - 1)

        GP.wait(sSp, sSp.n)

        with nc.Block() as block:
            @block.sync
            def _(eng):
                for f in E["sync"].ops:
                    f(eng)

            @block.gpsimd
            def _(eng):
                for f in E["gpsimd"].ops:
                    f(eng)

            @block.tensor
            def _(eng):
                for f in E["tensor"].ops:
                    f(eng)

            @block.vector
            def _(eng):
                for f in E["vector"].ops:
                    f(eng)

            @block.scalar
            def _(eng):
                for f in E["scalar"].ops:
                    f(eng)

    return nc


def _host_inputs(L0_attn_x, L0_attn_Wq, L0_attn_Wk, L0_attn_Wv, L0_attn_Wo,
                 L0_ffn_gamma, L0_ffn_W):
    x = np.asarray(L0_attn_x, dtype=np.float32).reshape(SEQ, D)
    wqt = np.ascontiguousarray(np.asarray(L0_attn_Wq, dtype=np.float32).T)
    wkt = np.ascontiguousarray(np.asarray(L0_attn_Wk, dtype=np.float32).T)
    wvt = np.ascontiguousarray(np.asarray(L0_attn_Wv, dtype=np.float32).T)
    wot = np.ascontiguousarray(np.asarray(L0_attn_Wo, dtype=np.float32).T)
    wft = np.ascontiguousarray(np.asarray(L0_ffn_W, dtype=np.float32).T)
    gam = np.tile(np.asarray(L0_ffn_gamma, dtype=np.float32).reshape(1, D), (128, 1))
    ident = np.eye(128, dtype=ml_dtypes.bfloat16)
    ones = np.ones((128, 128), dtype=np.float32)
    tcst = np.tile(np.array([[EPS_Q, EPS_RMS * EPS_LN, 2.0, EPS_LN]],
                            dtype=np.float32), (128, 1))

    inv_freq = (10000.0 ** (-(np.arange(0, HD, 2, dtype=np.float64) / HD))).astype(np.float64)

    in_maps = []
    for c in range(NCORES):
        xhc = np.zeros((ROWS, D), dtype=np.float32)
        lo = c * OWN - HALO
        if c > 0:
            xhc[:HALO] = x[lo:lo + HALO]
        xhc[HALO:] = x[c * OWN:(c + 1) * OWN]

        pos = np.arange(lo, lo + ROWS, dtype=np.float64)
        ang = pos[:, None] * inv_freq[None, :]
        cosv = np.cos(ang).astype(np.float32)
        sinv = np.sin(ang).astype(np.float32)
        if c == 0:
            cosv[:HALO] = 1.0
            sinv[:HALO] = 0.0
        s2 = np.empty((ROWS, HD), dtype=np.float32)
        s2[:, 0::2] = -sinv
        s2[:, 1::2] = sinv
        cos_sb = cosv.reshape(TIL, 128, 64).transpose(1, 0, 2).reshape(128, TIL * 64)
        s2_sb = s2.reshape(TIL, 128, HD).transpose(1, 0, 2).reshape(128, TIL * HD)

        m = np.full((4, 5, 128, 128), NEG, dtype=np.float32)
        a = np.arange(128)[:, None]
        b = np.arange(128)[None, :]
        for qt in range(4):
            for r in range(5):
                delta = (512 + qt * 128 + a) - ((qt + r) * 128 + b)
                jg = c * OWN - HALO + (qt + r) * 128 + b
                valid = (delta >= 0) & (delta <= 511) & (jg >= 0)
                m[qt, r] = np.where(valid, 0.0, NEG)
        mwin = np.where(a < b, 0.0, NEG).astype(np.float32)
        mcau = np.where(a >= b, 0.0, NEG).astype(np.float32)
        mask_sb = np.concatenate(
            [m.transpose(2, 0, 1, 3).reshape(128, 4 * 640), mwin, mcau],
            axis=1).astype(ml_dtypes.bfloat16)

        in_maps.append(dict(
            xh=xhc, wqt=wqt, wkt=wkt, wvt=wvt, wot=wot, wft=wft, gam=gam,
            cosd=cos_sb, s2d=s2_sb, maskd=mask_sb, identd=ident, onesd=ones,
            tcstd=tcst))
    return in_maps


def kernel(L0_attn_x, L0_attn_Wq, L0_attn_Wk, L0_attn_Wv, L0_attn_Wo,
           L0_ffn_gamma, L0_ffn_W):
    in_maps = _host_inputs(L0_attn_x, L0_attn_Wq, L0_attn_Wk, L0_attn_Wv,
                           L0_attn_Wo, L0_ffn_gamma, L0_ffn_W)
    nc = build_nc(debug=_DEBUG)
    res = run_bass_kernel_spmd(nc, in_maps, core_ids=list(range(NCORES)))
    out = np.concatenate([res.results[c]["o_x"] for c in range(NCORES)], axis=0)
    if _DEBUG:
        kernel._last_results = res
    return out.reshape(1, SEQ, D)


# version 14
# speedup vs baseline: 1.0101x; 1.0101x over previous
import math
import numpy as np
import ml_dtypes
import concourse.bass as bass
import concourse.mybir as mybir
import concourse.bass_utils as _bu
from concourse.bass_utils import run_bass_kernel_spmd

if not getattr(_bu, "_ldw_patched", False):
    _orig_run_command = _bu.run_command

    def _run_command_ldw(cmd, *a, **k):
        cmd = [c.replace("--enable-ldw-opt=false", "--enable-ldw-opt=true")
               if isinstance(c, str) else c for c in cmd]
        return _orig_run_command(cmd, *a, **k)

    _bu.run_command = _run_command_ldw
    _bu._ldw_patched = True

F32 = mybir.dt.float32
BF16 = mybir.dt.bfloat16
AX = mybir.AxisListType.X
ALU = mybir.AluOpType
ACT = mybir.ActivationFunctionType

NCORES = 8
SEQ = 8192
D = 2048
NH = 16
HD = 128
KC = D // 128
OWN = SEQ // NCORES
HALO = 512
ROWS = OWN + HALO
TIL = ROWS // 128
TOWN = OWN // 128
CMAGIC = 12582912.0
SCALE = float(HD) ** -0.5
EPS_LN = 1e-5
EPS_RMS = 1e-6
EPS_Q = 1e-5
NEG = -1.0e30

_DEBUG = False


class Ctr:
    __slots__ = ("h", "n")

    def __init__(self, h):
        self.h = h
        self.n = 0


class Eng:

    def __init__(self, name):
        self.name = name
        self.ops = []
        self._waited = {}

    def wait(self, ctr, val=None):
        v = ctr.n if val is None else val
        if v <= 0 or self._waited.get(id(ctr), -1) >= v:
            return
        self._waited[id(ctr)] = v
        self.ops.append(lambda e, h=ctr.h, vv=v: e.wait_ge(h, vv))

    def do(self, fn, inc=None, k=1):
        if inc is None:
            self.ops.append(fn)
        else:
            inc.n += k
            self.ops.append(lambda e, f=fn, h=inc.h, kk=k: f(e).then_inc(h, kk))


def build_nc(debug=False):
    nc = bass.Bass("TRN2", target_bir_lowering=False)

    EI = dict(kind="ExternalInput")
    xh = nc.dram_tensor("xh", [ROWS, D], F32, **EI).ap()
    wqt = nc.dram_tensor("wqt", [D, D], F32, **EI).ap()
    wkt = nc.dram_tensor("wkt", [D, D], F32, **EI).ap()
    wvt = nc.dram_tensor("wvt", [D, D], F32, **EI).ap()
    wot = nc.dram_tensor("wot", [D, D], F32, **EI).ap()
    wft = nc.dram_tensor("wft", [D, D], F32, **EI).ap()
    gam = nc.dram_tensor("gam", [128, D], F32, **EI).ap()
    cosd = nc.dram_tensor("cosd", [128, TIL * 64], F32, **EI).ap()
    s2d = nc.dram_tensor("s2d", [128, TIL * 128], F32, **EI).ap()
    maskd = nc.dram_tensor("maskd", [128, 4 * 640 + 256], BF16, **EI).ap()
    identd = nc.dram_tensor("identd", [128, 128], BF16, **EI).ap()
    onesd = nc.dram_tensor("onesd", [128, 128], F32, **EI).ap()
    tcstd = nc.dram_tensor("tcstd", [128, 4], F32, **EI).ap()

    SC = dict(kind="ExternalOutput") if debug else {}
    s_wi = nc.dram_tensor("s_wi", [128, KC * D], mybir.dt.uint16, **SC).ap()
    s_x1 = nc.dram_tensor("s_x1", [OWN, D], F32, **SC).ap()
    s_x2 = nc.dram_tensor("s_x2", [OWN, D], F32, **SC).ap()
    U16 = mybir.dt.uint16
    s_k = nc.dram_tensor("s_k", [ROWS, D], U16, **SC).ap()
    s_v = nc.dram_tensor("s_v", [ROWS, D], U16, **SC).ap()
    s_q = nc.dram_tensor("s_q", [OWN, D], U16, **SC).ap()
    o_x = nc.dram_tensor("o_x", [OWN, D], F32, kind="ExternalOutput").ap()
    d_dbg = nc.dram_tensor("d_dbg", [128, 24], F32, kind="ExternalOutput").ap()

    from contextlib import ExitStack
    with ExitStack() as ctx:
        def sb(name, shape, dt):
            return ctx.enter_context(nc.sbuf_tensor(name, shape, dt))

        big0 = sb("big0", [128, KC * D], BF16)
        big1 = sb("big1", [128, TIL * D], BF16)
        gam_s = sb("gam_s", [128, D], F32)
        cos_s = sb("cos_s", [128, TIL * 64], F32)
        s2_s = sb("s2_s", [128, TIL * 128], F32)
        mask_s = sb("mask_s", [128, 4 * 640 + 256], BF16)
        ident = sb("ident", [128, 128], BF16)
        ones = sb("ones", [128, 128], F32)
        tcst = sb("tcst", [128, 4], F32)
        xbuf = [sb("xbuf0", [128, D], F32), sb("xbuf1", [128, D], F32)]
        x1b = sb("x1b", [128, D], F32)
        sA = sb("sA", [128, D], F32)
        sB = sb("sB", [128, D], F32)
        xi = [sb("xi0", [128, D], BF16), sb("xi1", [128, D], BF16)]
        xiT = sb("xiT", [128, D], BF16)
        xn2tmp = sb("xn2tmp", [128, D], BF16)
        krot = [sb("krot0", [128, D // 2], BF16), sb("krot1", [128, D // 2], BF16)]
        kfm = [sb("kfm0", [128, D // 2], BF16), sb("kfm1", [128, D // 2], BF16)]
        pbf = [sb("pbf0", [128, 640], BF16), sb("pbf1", [128, 640], BF16)]
        diagb = [sb("diagb0", [128, 128], BF16), sb("diagb1", [128, 128], BF16)]

        def tiny(name):
            return sb(name, [128, 1], F32)

        wparts = sb("wparts", [128, KC], F32)
        t_wsum = tiny("t_wsum")
        t_wtot = tiny("t_wtot")
        t_tw = tiny("t_tw")
        t_wl = tiny("t_wl")
        t_wr0 = tiny("t_wr0")
        t_wu = tiny("t_wu")
        t_wu2 = tiny("t_wu2")
        t_ws = tiny("t_ws")
        def tpair(name):
            return [tiny(name + "0"), tiny(name + "1")]

        FT = {n: tpair("f_" + n) for n in
              ("sum", "ssq", "m", "v1", "mm", "var", "arg", "ln", "inv", "b",
               "nb", "am", "t", "l2", "r0", "u", "u2", "r1", "a", "s1")}
        FT["rs"] = [tiny("f_rs0"), tiny("f_rs1"), tiny("f_rs2")]
        UT = {n: tpair("u_" + n) for n in
              ("sum", "ssq", "m", "v1", "mm", "var", "arg", "ln", "inv", "b", "nb")}
        t_s1 = tiny("t_s1")
        t_s2 = tiny("t_s2")
        C_nm = [tiny("C_nm%d" % i) for i in range(4)]
        C_rs = [tiny("C_rs%d" % i) for i in range(4)]
        C_rl = [tiny("C_rl%d" % i) for i in range(4)]
        C_ri = [tiny("C_ri%d" % i) for i in range(4)]

        psA = ctx.enter_context(nc.psum_tensor("psA", [128, D], F32))
        psB = ctx.enter_context(nc.psum_tensor("psB", [128, D], F32))

        sLx = Ctr(ctx.enter_context(nc.semaphore("sLx")))
        sLw = Ctr(ctx.enter_context(nc.semaphore("sLw")))
        sLr = Ctr(ctx.enter_context(nc.semaphore("sLr")))
        sG = Ctr(ctx.enter_context(nc.semaphore("sG")))
        sSp = Ctr(ctx.enter_context(nc.semaphore("sSp")))
        sT = Ctr(ctx.enter_context(nc.semaphore("sT")))
        sV = Ctr(ctx.enter_context(nc.semaphore("sV")))
        sC = Ctr(ctx.enter_context(nc.semaphore("sC")))

        E = {n: Eng(n) for n in ("sync", "gpsimd", "tensor", "vector", "scalar")}
        SY, GP, TE, VE, SE = (E[n] for n in ("sync", "gpsimd", "tensor", "vector", "scalar"))

        for dst, src in ((gam_s, gam), (cos_s, cosd), (s2_s, s2d),
                         (mask_s, maskd), (ident, identd), (ones, onesd),
                         (tcst, tcstd)):
            SY.do(lambda e, d=dst, s=src: e.dma_start(out=d[:, :], in_=s), sLw, 16)
        consts_ld = sLw.n

        big0f = big0[:, :].bitcast(F32)
        big1f = big1[:, :].bitcast(F32)

        def stage_ap(c):
            if c < 8:
                return big0f[:, c * D:(c + 1) * D]
            if c < 14:
                return big1f[:, (c - 8) * D:(c - 7) * D]
            return xbuf[c - 14][:, :]

        def dma_rows(e, dst, c0, c1):
            src = xh
            return None

        SY.do(lambda e: e.dma_start(
            out=big0f[:, 0:8 * D].rearrange("p (c j) -> p c j", c=8),
            in_=wft[0:1024, :].rearrange("(c p) j -> p c j", c=8)), sLw, 16)
        SY.do(lambda e: e.dma_start(
            out=big1f[:, 0:6 * D].rearrange("p (c j) -> p c j", c=6),
            in_=wft[1024:1792, :].rearrange("(c p) j -> p c j", c=6)), sLw, 16)
        SY.do(lambda e: e.dma_start(out=xbuf[0][:, :], in_=wft[1792:1920, :]), sLw, 16)
        SY.do(lambda e: e.dma_start(out=xbuf[1][:, :], in_=wft[1920:2048, :]), sLw, 16)
        wst_ld = sLw.n

        SE.wait(sLw, wst_ld)
        for c in range(KC):
            SE.do(lambda e, c=c: e.activation(
                out=xi[0][:, :], in_=stage_ap(c), func=ACT.Abs,
                accum_out=wparts[:, c:c + 1]), sC)
        VE.wait(sC, sC.n)
        VE.do(lambda e: e.reduce_sum(t_wsum[:, :], wparts[:, :], axis=AX), sV)
        TE.wait(sV, sV.n)
        TE.wait(sLw, consts_ld)
        TE.do(lambda e: e.matmul(psA[:, 0:1], ones[:, :], t_wsum[:, :],
                                 start=True, stop=True), sT)
        SE.wait(sT, sT.n)
        SE.wait(sLw, consts_ld)
        SE.do(lambda e: e.activation(out=t_wtot[:, :], in_=psA[:, 0:1], func=ACT.Identity))
        SE.do(lambda e: e.activation(out=t_tw[:, :], in_=t_wtot[:, :], func=ACT.Identity,
                                     scale=1.0 / (D * D), bias=tcst[:, 0:1]))
        SE.do(lambda e: e.activation(out=t_wl[:, :], in_=t_tw[:, :], func=ACT.Ln))
        SE.do(lambda e: e.activation(out=t_wr0[:, :], in_=t_wl[:, :],
                                     func=ACT.Exp, scale=-1.0))
        SE.do(lambda e: e.activation(out=t_wu[:, :], in_=t_tw[:, :], func=ACT.Identity,
                                     scale=t_wr0[:, :]))
        SE.do(lambda e: e.activation(out=t_wu2[:, :], in_=t_wu[:, :], func=ACT.Identity,
                                     scale=-1.0, bias=tcst[:, 2:3]))
        SE.do(lambda e: e.activation(out=t_ws[:, :], in_=t_wu2[:, :], func=ACT.Identity,
                                     scale=t_wr0[:, :]), sC)
        VE.wait(sC, sC.n)
        for c in range(KC):
            VE.do(lambda e, c=c: e.tensor_scalar(sA[:, :], stage_ap(c), t_ws[:, :],
                                                 CMAGIC, ALU.mult, ALU.add))
            VE.do(lambda e: e.tensor_scalar(sB[:, :], sA[:, :], CMAGIC, None,
                                            ALU.subtract))
            VE.do(lambda e, c=c: e.tensor_scalar(big0[:, c * D:(c + 1) * D], sB[:, :],
                                                 1.0, -1.0, ALU.min, ALU.max), sV)
        quant_done_v = sV.n
        GP.wait(sV, quant_done_v)
        GP.do(lambda e: e.dma_start(out=s_wi.bitcast(BF16), in_=big0[:, :]), sSp, 16)
        wi_spill = sSp.n
        GP.do(lambda e: e.dma_start(out=d_dbg[:, 0:16], in_=wparts[:, :]), sSp, 16)
        for ii, tb in enumerate((t_wsum, t_wtot, t_tw, t_ws, t_wr0, t_wu, t_wu2)):
            GP.do(lambda e, ii=ii, tb=tb: e.dma_start(
                out=d_dbg[:, 16 + ii:17 + ii], in_=tb[:, :],
                allow_slow_non_contiguous=True), sSp, 16)

        Ast = {}

        def ffn_prep(t, i, xsrc):
            p = t % 2
            VE.do(lambda e: e.reduce_sum(FT["sum"][p][:, :], xsrc[:, :], axis=AX), sV)
            SE.wait(sV, sV.n)
            if ("tr", t - 2) in Ast:
                SE.wait(sT, Ast[("tr", t - 2)])
            SE.do(lambda e: e.activation(out=xi[i][:, :], in_=xsrc[:, :],
                                         func=ACT.Square, accum_out=FT["ssq"][p][:, :]))
            SE.do(lambda e: e.activation(out=FT["m"][p][:, :], in_=FT["sum"][p][:, :],
                                         func=ACT.Identity, scale=1.0 / D))
            SE.do(lambda e: e.activation(out=FT["v1"][p][:, :], in_=FT["ssq"][p][:, :],
                                         func=ACT.Identity, scale=1.0 / D))
            SE.do(lambda e: e.activation(out=FT["mm"][p][:, :], in_=FT["m"][p][:, :],
                                         func=ACT.Identity, scale=FT["m"][p][:, :]))
            SE.do(lambda e: e.activation(out=FT["var"][p][:, :], in_=FT["mm"][p][:, :],
                                         func=ACT.Identity, scale=-1.0, bias=FT["v1"][p][:, :]))
            SE.do(lambda e: e.activation(out=FT["arg"][p][:, :], in_=FT["var"][p][:, :],
                                         func=ACT.Identity, scale=1.0 + EPS_RMS,
                                         bias=tcst[:, 1:2]))
            SE.do(lambda e: e.activation(out=FT["ln"][p][:, :], in_=FT["arg"][p][:, :], func=ACT.Ln))
            SE.do(lambda e: e.activation(out=FT["inv"][p][:, :], in_=FT["ln"][p][:, :],
                                         func=ACT.Exp, scale=-0.5))
            SE.do(lambda e: e.activation(out=FT["b"][p][:, :], in_=FT["m"][p][:, :],
                                         func=ACT.Identity, scale=FT["inv"][p][:, :]))
            SE.do(lambda e: e.activation(out=FT["nb"][p][:, :], in_=FT["b"][p][:, :],
                                         func=ACT.Identity, scale=-1.0), sC)
            VE.wait(sC, sC.n)
            VE.do(lambda e: e.tensor_scalar(sA[:, :], xsrc[:, :], FT["inv"][p][:, :],
                                            FT["nb"][p][:, :], ALU.mult, ALU.add))
            VE.do(lambda e: e.tensor_tensor(out=sB[:, :], in0=sA[:, :],
                                            in1=gam_s[:, :], op=ALU.mult))
            VE.do(lambda e: e.reduce_max(FT["am"][p][:, :], sB[:, :], axis=AX,
                                         apply_absolute_value=True), sV)
            SE.wait(sV, sV.n)
            SE.do(lambda e: e.activation(out=FT["t"][p][:, :], in_=FT["am"][p][:, :],
                                         func=ACT.Identity, bias=tcst[:, 0:1]))
            SE.do(lambda e: e.activation(out=FT["l2"][p][:, :], in_=FT["t"][p][:, :], func=ACT.Ln))
            SE.do(lambda e: e.activation(out=FT["r0"][p][:, :], in_=FT["l2"][p][:, :],
                                         func=ACT.Exp, scale=-1.0))
            SE.do(lambda e: e.activation(out=FT["u"][p][:, :], in_=FT["t"][p][:, :],
                                         func=ACT.Identity, scale=FT["r0"][p][:, :]))
            SE.do(lambda e: e.activation(out=FT["u2"][p][:, :], in_=FT["u"][p][:, :],
                                         func=ACT.Identity, scale=-1.0, bias=tcst[:, 2:3]))
            SE.do(lambda e: e.activation(out=FT["r1"][p][:, :], in_=FT["u2"][p][:, :],
                                         func=ACT.Identity, scale=FT["r0"][p][:, :]))
            SE.do(lambda e: e.activation(out=FT["a"][p][:, :], in_=FT["r1"][p][:, :],
                                         func=ACT.Identity, scale=127.0))
            SE.do(lambda e: e.activation(out=FT["s1"][p][:, :], in_=FT["t"][p][:, :],
                                         func=ACT.Identity, scale=t_tw[:, :]))
            SE.do(lambda e, t=t: e.activation(out=FT["rs"][t % 3][:, :], in_=FT["s1"][p][:, :],
                                         func=ACT.Identity, scale=0.5 / 127.0), sC)
            VE.wait(sC, sC.n)
            VE.do(lambda e: e.tensor_scalar(sA[:, :], sB[:, :], FT["a"][p][:, :],
                                            CMAGIC, ALU.mult, ALU.add))
            if ("tr", t - 2) in Ast:
                VE.wait(sT, Ast[("tr", t - 2)])
            VE.do(lambda e, i=i: e.tensor_scalar(xi[i][:, :], sA[:, :], CMAGIC, None,
                                                 ALU.subtract), sV)
            Ast[("quant", t)] = sV.n

        def ffnb(t):
            return (krot[0], krot[1]) if t % 2 == 0 else (kfm[0], kfm[1])

        def ffn_mm(t, i):
            p = t % 2
            TE.wait(sV, Ast[("quant", t)])
            if ("evacxn2", t - 2) in Ast:
                TE.wait(sC, Ast[("evacxn2", t - 2)])
            for c in range(KC):
                TE.do(lambda e, c=c, i=i: e.matmul(
                    psA[:, c * 128:(c + 1) * 128], xi[i][:, c * 128:(c + 1) * 128],
                    ident[:, :], start=True, stop=True), sT if c == KC - 1 else None)
            Ast[("tr", t)] = sT.n
            SE.wait(sT, sT.n)
            SE.do(lambda e: e.activation(out=xiT[:, :], in_=psA[:, :], func=ACT.Copy), sC)
            Ast[("evacxiT", t)] = sC.n
            TE.wait(sC, sC.n)
            if ("evacffn", t - 1) in Ast:
                TE.wait(sC, Ast[("evacffn", t - 1)])
            for c in range(KC):
                for ss in range(4):
                    TE.do(lambda e, c=c, ss=ss: e.matmul(
                        psB[:, ss * 512:(ss + 1) * 512],
                        xiT[:, c * 128:(c + 1) * 128],
                        big0[:, c * D + ss * 512: c * D + (ss + 1) * 512],
                        start=(c == 0), stop=(c == KC - 1)),
                        sT if (c == KC - 1 and ss == 3) else None)
            Ast[("mm", t)] = sT.n

        def emit_evacffn(t):
            p = t % 2
            fb = ffnb(t)
            SE.wait(sT, Ast[("mm", t)])
            if ("x1", t - 2) in Ast:
                SE.wait(sV, Ast[("x1", t - 2)])
            SE.do(lambda e, fb=fb, t=t: e.activation(
                out=fb[0][:, :], in_=psB[:, 0:1024], func=ACT.Identity,
                scale=FT["rs"][t % 3][:, :]))
            SE.do(lambda e, fb=fb, t=t: e.activation(
                out=fb[1][:, :], in_=psB[:, 1024:2048], func=ACT.Identity,
                scale=FT["rs"][t % 3][:, :]), sC)
            Ast[("evacffn", t)] = sC.n

        def emit_resid(t, xsrc, xdst):
            fb = ffnb(t)
            VE.wait(sC, Ast[("evacffn", t)])
            VE.do(lambda e, fb=fb, xsrc=xsrc, xdst=xdst: e.tensor_tensor(
                out=xdst[:, 0:1024], in0=xsrc[:, 0:1024], in1=fb[0][:, :], op=ALU.add))
            VE.do(lambda e, fb=fb, xsrc=xsrc, xdst=xdst: e.tensor_tensor(
                out=xdst[:, 1024:2048], in0=xsrc[:, 1024:2048], in1=fb[1][:, :],
                op=ALU.add), sV)
            Ast[("x1", t)] = sV.n

        x1_spills = {}

        def xslot(t):
            return (xbuf[0], xbuf[1], x1b)[t % 3]

        def emit_xn2tr(t):
            TE.wait(sV, Ast[("xn2", t)])
            key = ("evacxiT", t + 2) if ("evacxiT", t + 2) in Ast else (
                ("evacxiT", t + 1) if ("evacxiT", t + 1) in Ast else ("evacxiT", t))
            TE.wait(sC, Ast[key])
            for c in range(KC):
                TE.do(lambda e, c=c: e.matmul(
                    psA[:, c * 128:(c + 1) * 128], xn2tmp[:, c * 128:(c + 1) * 128],
                    ident[:, :], start=True, stop=True), sT if c == KC - 1 else None)
            Ast[("xn2tr", t)] = sT.n
            SE.wait(sT, sT.n)
            SE.do(lambda e, t=t: e.activation(out=big1[:, t * D:(t + 1) * D],
                                              in_=psA[:, :], func=ACT.Copy), sC)
            Ast[("evacxn2", t)] = sC.n

        def emit_A_tail(t):
            p = t % 2
            xs = xslot(t)
            emit_resid(t, xs, xs)
            if t >= 4:
                GP.wait(sV, Ast[("x1", t)])
                GP.do(lambda e, t=t, xs=xs: e.dma_start(
                    out=s_x1[(t - 4) * 128:(t - 3) * 128, :], in_=xs[:, :]), sSp, 16)
                x1_spills[t] = sSp.n
            VE.do(lambda e, xs=xs: e.reduce_sum(UT["sum"][p][:, :], xs[:, :],
                                                axis=AX), sV)
            SE.wait(sV, sV.n)
            if ("xn2tr", t - 1) in Ast:
                SE.wait(sT, Ast[("xn2tr", t - 1)])
            SE.do(lambda e, xs=xs: e.activation(out=xn2tmp[:, :], in_=xs[:, :],
                                                func=ACT.Square, accum_out=UT["ssq"][p][:, :]))
            SE.do(lambda e: e.activation(out=UT["m"][p][:, :], in_=UT["sum"][p][:, :],
                                         func=ACT.Identity, scale=1.0 / D))
            SE.do(lambda e: e.activation(out=UT["v1"][p][:, :], in_=UT["ssq"][p][:, :],
                                         func=ACT.Identity, scale=1.0 / D))
            SE.do(lambda e: e.activation(out=UT["mm"][p][:, :], in_=UT["m"][p][:, :],
                                         func=ACT.Identity, scale=UT["m"][p][:, :]))
            SE.do(lambda e: e.activation(out=UT["var"][p][:, :], in_=UT["mm"][p][:, :],
                                         func=ACT.Identity, scale=-1.0, bias=UT["v1"][p][:, :]))
            SE.do(lambda e: e.activation(out=UT["arg"][p][:, :], in_=UT["var"][p][:, :],
                                         func=ACT.Identity, bias=tcst[:, 3:4]))
            SE.do(lambda e: e.activation(out=UT["ln"][p][:, :], in_=UT["arg"][p][:, :], func=ACT.Ln))
            SE.do(lambda e: e.activation(out=UT["inv"][p][:, :], in_=UT["ln"][p][:, :],
                                         func=ACT.Exp, scale=-0.5))
            SE.do(lambda e: e.activation(out=UT["b"][p][:, :], in_=UT["m"][p][:, :],
                                         func=ACT.Identity, scale=UT["inv"][p][:, :]))
            SE.do(lambda e: e.activation(out=UT["nb"][p][:, :], in_=UT["b"][p][:, :],
                                         func=ACT.Identity, scale=-1.0), sC)
            VE.wait(sC, sC.n)
            if ("xn2tr", t - 1) in Ast:
                VE.wait(sT, Ast[("xn2tr", t - 1)])
            VE.do(lambda e, xs=xs: e.tensor_scalar(xn2tmp[:, :], xs[:, :],
                                                   UT["inv"][p][:, :], UT["nb"][p][:, :],
                                                   ALU.mult, ALU.add), sV)
            Ast[("xn2", t)] = sV.n

        SY.wait(sV, quant_done_v)
        SY.do(lambda e: e.dma_start(out=xbuf[0][:, :], in_=xh[0:128, :]), sLx, 16)
        Ast[("xld", 0)] = sLx.n
        for t in range(TIL):
            i = t % 2
            VE.wait(sLx, Ast[("xld", t)])
            SE.wait(sLx, Ast[("xld", t)])
            ffn_prep(t, i, xslot(t))
            if t >= 2:
                emit_evacffn(t - 2)
            if t >= 1:
                ffn_mm(t - 1, (t - 1) % 2)
            if t >= 2:
                emit_A_tail(t - 2)
                emit_xn2tr(t - 2)
            if t + 1 < TIL:
                if t + 1 <= 2:
                    SY.wait(sV, quant_done_v)
                if ("x1", t - 2) in Ast:
                    SY.wait(sV, Ast[("xn2", t - 2)])
                    if t - 2 >= 4:
                        SY.wait(sSp, x1_spills[t - 2])
                SY.do(lambda e, t1=t + 1: e.dma_start(
                    out=xslot(t1)[:, :], in_=xh[t1 * 128:(t1 + 1) * 128, :]), sLx, 16)
                Ast[("xld", t + 1)] = sLx.n
        emit_evacffn(TIL - 2)
        ffn_mm(TIL - 1, (TIL - 1) % 2)
        emit_A_tail(TIL - 2)
        emit_xn2tr(TIL - 2)
        emit_evacffn(TIL - 1)
        emit_A_tail(TIL - 1)
        emit_xn2tr(TIL - 1)

        HW = KC * D // 2
        subs = []
        for wsrc, kind in ((wkt, "k"), (wvt, "v"), (wqt, "q")):
            for hf in range(2):
                subs.append((wsrc, kind, hf))
        sub_done_T = [0] * len(subs)
        TE.wait(sC, sC.n)

        def emit_rope_or_evac(kind, t, tp, hf):
            po = tp * 1024
            if kind == "v":
                SE.wait(sT, Bst["mm"][t])
                if t - 2 in Bst["spill"]:
                    SE.wait(sSp, Bst["spill"][t - 2])
                SE.do(lambda e, tp=tp: e.activation(
                    out=krot[tp][:, :], in_=psA[:, tp * 1024:(tp + 1) * 1024],
                    func=ACT.Copy), sC)
                Bst["rope"][t] = ("C", sC.n)
            else:
                VE.wait(sT, Bst["mm"][t])
                if t - 2 in Bst["tr"]:
                    VE.wait(sT, Bst["tr"][t - 2])
                c2 = bass.AP(cos_s, t * 64,
                             [[TIL * 64, 128], [0, 8], [1, 64], [0, 2]])
                s2 = bass.AP(s2_s, t * 128,
                             [[TIL * 128, 128], [0, 8], [1, 128]])
                qv = bass.AP(psA, po, [[D, 128], [128, 8], [2, 64], [1, 2]])
                qsw = bass.AP(psA, po + 1, [[D, 128], [128, 8], [2, 64], [-1, 2]])
                sAh = bass.AP(sA, 0, [[D, 128], [128, 8], [2, 64], [1, 2]])
                sBh = bass.AP(sB, 0, [[D, 128], [128, 8], [2, 64], [1, 2]])
                VE.do(lambda e, qv=qv, c2=c2, sAh=sAh: e.tensor_tensor(
                    out=sAh, in0=qv, in1=c2, op=ALU.mult))
                VE.do(lambda e, qsw=qsw, s2=s2, sBh=sBh: e.tensor_tensor(
                    out=sBh, in0=qsw, in1=s2, op=ALU.mult))
                VE.do(lambda e, tp=tp: e.tensor_tensor(
                    out=krot[tp][:, :], in0=sA[:, 0:1024], in1=sB[:, 0:1024],
                    op=ALU.add), sV)
                Bst["rope"][t] = ("V", sV.n)

        def emit_tr_evac_spill(kind, t, tp, hf):
            if kind == "v":
                eng, val = Bst["rope"][t]
                GP.wait(sC, val)
                GP.do(lambda e, t=t, hf=hf, tp=tp: e.dma_start(
                    out=s_v[t * 128:(t + 1) * 128,
                            hf * 1024:(hf + 1) * 1024].bitcast(BF16),
                    in_=krot[tp][:, :]), sSp, 16)
                Bst["spill"][t] = sSp.n
                return
            eng, val = Bst["rope"][t]
            TE.wait(sV, val)
            if t - 1 in Bst["evac"]:
                TE.wait(sC, Bst["evac"][t - 1])
            for hh in range(8):
                TE.do(lambda e, hh=hh, tp=tp: e.matmul(
                    psB[:, hh * 128:(hh + 1) * 128],
                    krot[tp][:, hh * 128:(hh + 1) * 128],
                    ident[:, :], start=True, stop=True),
                    sT if hh == 7 else None)
            Bst["tr"][t] = sT.n
            SE.wait(sT, Bst["tr"][t])
            if t - 2 in Bst["spill"]:
                SE.wait(sSp, Bst["spill"][t - 2])
            kscale = SCALE if kind == "k" else 1.0
            SE.do(lambda e, tp=tp, ks=kscale: e.activation(
                out=kfm[tp][:, :], in_=psB[:, 0:1024], func=ACT.Identity,
                scale=ks), sC)
            Bst["evac"][t] = sC.n
            GP.wait(sC, Bst["evac"][t])
            dst = s_k if kind == "k" else s_q
            toff = t if kind == "k" else t - 4
            GP.do(lambda e, dst=dst, toff=toff, hf=hf, tp=tp: e.dma_start(
                out=dst[toff * 128:(toff + 1) * 128,
                        hf * 1024:(hf + 1) * 1024].bitcast(BF16),
                in_=kfm[tp][:, :]), sSp, 16)
            Bst["spill"][t] = sSp.n

        def emit_wload(si):
            wsrc, kind, hf = subs[si]
            half = si % 2
            if si < 2:
                GP.wait(sT, sT.n)
            else:
                GP.wait(sT, sub_done_T[si - 2])
            GP.do(lambda e, w=wsrc, hf=hf, half=half: e.dma_start(
                out=big0[:, half * HW:(half + 1) * HW].rearrange("p (c j) -> p c j", c=KC),
                in_=w[:, hf * (D // 2):(hf + 1) * (D // 2)]
                    .rearrange("(c p) j -> p c j", c=KC)), sG, 16)
            return sG.n

        wload_done = [0] * len(subs)
        wload_done[0] = emit_wload(0)
        wload_done[1] = emit_wload(1)
        for si, (wsrc, kind, hf) in enumerate(subs):
            half = si % 2
            Bst = {"mm": {}, "rope": {}, "tr": {}, "evac": {}, "spill": {}}
            tiles = list(range(TIL)) if kind in ("k", "v") else list(range(4, TIL))
            TE.wait(sG, wload_done[si])
            TE.wait(sV, sV.n)
            TE.wait(sC, sC.n)
            VE.wait(sSp, sSp.n)
            SE.wait(sSp, sSp.n)
            prev = None
            for t in tiles:
                tp = t % 2
                if t - 2 in Bst["rope"]:
                    eng, val = Bst["rope"][t - 2]
                    if eng == "V":
                        TE.wait(sV, val)
                    else:
                        TE.wait(sC, val)
                for c in range(KC):
                    for ss in range(2):
                        TE.do(lambda e, c=c, ss=ss, t=t, half=half, tp=tp: e.matmul(
                            psA[:, tp * 1024 + ss * 512: tp * 1024 + (ss + 1) * 512],
                            big1[:, t * D + c * 128:t * D + (c + 1) * 128],
                            big0[:, half * HW + c * 1024 + ss * 512:
                                    half * HW + c * 1024 + (ss + 1) * 512],
                            start=(c == 0), stop=(c == KC - 1)),
                            sT if (c == KC - 1 and ss == 1) else None)
                Bst["mm"][t] = sT.n
                emit_rope_or_evac(kind, t, tp, hf)
                if prev is not None:
                    emit_tr_evac_spill(kind, prev, prev % 2, hf)
                prev = t
            emit_tr_evac_spill(kind, prev, prev % 2, hf)
            sub_done_T[si] = sT.n
            if si + 2 < len(subs):
                wload_done[si + 2] = emit_wload(si + 2)

        qsl = sA[:, :].bitcast(BF16)

        def kslot(bt):
            return big1[:, (bt % 6) * D:((bt % 6) + 1) * D]

        def vslot(bt):
            return big1[:, (6 + bt % 6) * D:(7 + bt % 6) * D]

        def qslot(qt):
            return qsl[:, (qt % 2) * D:((qt % 2) + 1) * D]

        def x1slot(qt):
            return (x1b if qt % 2 == 0 else sB)

        attnfm = xi[0]
        x2b = xbuf[0]

        GP.wait(sT, sT.n)
        GP.do(lambda e: e.dma_start(
            out=big0[:, :].rearrange("p (c j) -> p c j", c=KC),
            in_=wot.rearrange("(c p) j -> p c j", c=KC)), sG, 16)
        wo_ld = sG.n

        SY.wait(sT, sT.n)
        SY.wait(sV, sV.n)
        SY.wait(sSp, sSp.n)
        for bt in range(6):
            SY.do(lambda e, bt=bt: e.dma_start(
                out=kslot(bt), in_=s_k[bt * 128:(bt + 1) * 128, :].bitcast(BF16)), sLr, 16)
            SY.do(lambda e, bt=bt: e.dma_start(
                out=vslot(bt), in_=s_v[bt * 128:(bt + 1) * 128, :].bitcast(BF16)), sLr, 16)
        for q0 in (0, 1):
            SY.do(lambda e, q0=q0: e.dma_start(
                out=qslot(q0), in_=s_q[q0 * 128:(q0 + 1) * 128, :].bitcast(BF16)), sLr, 16)
        x1_ld = {}
        for q0 in (0, 1):
            SY.wait(sSp, x1_spills[q0 + 4])
            SY.do(lambda e, q0=q0: e.dma_start(
                out=x1slot(q0)[:, :], in_=s_x1[q0 * 128:(q0 + 1) * 128, :]), sLx, 16)
            x1_ld[q0] = sLx.n
        ring_ready = {0: sLr.n}

        NSUP = NH // 2
        SLOT = [(psB, 0), (psA, 512)]

        def sblk(par, j, r=None, n=640):
            tns, base = SLOT[par]
            off = base + j * 640 + (0 if r is None else r * 128)
            ln = n if r is None else 128
            return tns[:, off:off + ln]

        def s2h_ap(par, off, blocklen):
            tns, base = SLOT[par]
            return bass.AP(tns, base + off, [[2048, 128], [640, 2], [1, blocklen]])

        psO = psA[:, 0:256]
        psW = psA[:, 1024:2048]
        pbfset = [(pbf[0], pbf[1]), (kfm[0][:, 0:640], kfm[1][:, 0:640])]
        diagset = [(diagb[0], krot[0][:, 0:128]), (diagb[1], krot[1][:, 0:128])]

        evacPT_done = [0, 0]
        sc_done = {}
        vmax_done = {}
        recip_done = {}
        diag_done = {}
        pt_done = {}
        av_done = {}
        x2_spill = {}
        sc_last = {}
        x2evac_done = {}
        evacOh_done = {}
        ring_rd = {0: ring_ready[0]}
        psO_c = [psA[:, 0:256], psA[:, 1792:2048]]
        psWq = psB[:, 1536:2048]
        attnfm_c = [xi[0], xi[1]]

        def c_scores(qt, m, ch):
            g = qt * NSUP + m
            h0 = 2 * m
            TE.wait(sLr, ring_rd[qt // 2])
            if evacPT_done[ch]:
                TE.wait(sV, evacPT_done[ch])
            if ch == 1:
                pg0 = g - 2 if m >= 2 else (qt - 2) * NSUP + NSUP - 2 + m
                if pg0 in evacOh_done:
                    TE.wait(sC, evacOh_done[pg0])
            for j in range(2):
                for r in range(5):
                    TE.do(lambda e, h=h0 + j, r=r, qt=qt, ch=ch, j=j: e.matmul(
                        sblk(ch, j, r),
                        qslot(qt)[:, h * 128:(h + 1) * 128],
                        kslot(qt + r)[:, h * 128:(h + 1) * 128],
                        start=True, stop=True),
                        sT if (j == 1 and r == 4) else None)
            sc_done[g] = sT.n
            if m == NSUP - 1:
                sc_last[qt] = sT.n
            VE.wait(sT, sc_done[g])
            if qt < 4:
                VE.do(lambda e, ch=ch, qt=qt: e.tensor_tensor(
                    out=s2h_ap(ch, 0, 640), in0=s2h_ap(ch, 0, 640),
                    in1=bass.AP(mask_s, qt * 640,
                                [[2816, 128], [0, 2], [1, 640]]),
                    op=ALU.add))
            else:
                VE.do(lambda e, ch=ch: e.tensor_tensor(
                    out=s2h_ap(ch, 0, 128), in0=s2h_ap(ch, 0, 128),
                    in1=bass.AP(mask_s, 2560, [[2816, 128], [0, 2], [1, 128]]),
                    op=ALU.add))
                VE.do(lambda e, ch=ch: e.tensor_tensor(
                    out=s2h_ap(ch, 512, 128), in0=s2h_ap(ch, 512, 128),
                    in1=bass.AP(mask_s, 2688, [[2816, 128], [0, 2], [1, 128]]),
                    op=ALU.add))
            for j in range(2):
                VE.do(lambda e, ch=ch, j=j: e.reduce_max(
                    C_nm[ch * 2 + j][:, :], sblk(ch, j), axis=AX,
                    negate=True), sV if j == 1 else None)
            vmax_done[g] = sV.n
            SE.wait(sV, vmax_done[g])
            pg = g - 1 if m > 0 else (qt - 2) * NSUP + NSUP - 1
            if pg in av_done:
                SE.wait(sT, av_done[pg])
            for j in range(2):
                SE.do(lambda e, ch=ch, j=j: e.activation(
                    out=pbfset[ch][j][:, :], in_=sblk(ch, j), func=ACT.Exp,
                    bias=C_nm[ch * 2 + j][:, :], scale=1.0,
                    accum_out=C_rs[ch * 2 + j][:, :]))
            for j in range(2):
                SE.do(lambda e, ch=ch, j=j: e.activation(
                    out=C_rl[ch * 2 + j][:, :], in_=C_rs[ch * 2 + j][:, :],
                    func=ACT.Ln))
                SE.do(lambda e, ch=ch, j=j: e.activation(
                    out=C_ri[ch * 2 + j][:, :], in_=C_rl[ch * 2 + j][:, :],
                    func=ACT.Exp, scale=-1.0),
                    sC if j == 1 else None)
            recip_done[g] = sC.n

        def c_ptav(qt, m, ch):
            g = qt * NSUP + m
            VE.wait(sC, recip_done[g])
            pg = g - 1 if m > 0 else (qt - 2) * NSUP + NSUP - 1
            if pg in pt_done:
                VE.wait(sT, pt_done[pg])
            for j in range(2):
                VE.do(lambda e, ch=ch, j=j: e.tensor_scalar(
                    diagset[ch][j][:, :], ident[:, :],
                    C_ri[ch * 2 + j][:, :], None, ALU.mult),
                    sV if j == 1 else None)
            diag_done[g] = sV.n
            TE.wait(sV, diag_done[g])
            if ch == 1:
                pgO0 = g - 1 if m > 0 else (qt - 2) * NSUP + NSUP - 1
                if pgO0 in evacOh_done:
                    TE.wait(sC, evacOh_done[pgO0])
            for j in range(2):
                for r in range(5):
                    TE.do(lambda e, r=r, ch=ch, j=j: e.matmul(
                        sblk(ch, j, r),
                        pbfset[ch][j][:, r * 128:(r + 1) * 128],
                        diagset[ch][j][:, :], start=True, stop=True),
                        sT if (j == 1 and r == 4) else None)
            pt_done[g] = sT.n
            VE.wait(sT, pt_done[g])
            for j in range(2):
                VE.do(lambda e, ch=ch, j=j: e.tensor_copy(
                    out=pbfset[ch][j][:, :], in_=sblk(ch, j)),
                    sV if j == 1 else None)
            evacPT_done[ch] = sV.n
            pgO = g - 1 if m > 0 else (qt - 2) * NSUP + NSUP - 1
            if pgO in evacOh_done:
                TE.wait(sC, evacOh_done[pgO])
            TE.wait(sV, evacPT_done[ch])
            for j in range(2):
                for r in range(5):
                    TE.do(lambda e, h=2 * m + j, r=r, qt=qt, ch=ch, j=j: e.matmul(
                        psO_c[ch][:, j * 128:(j + 1) * 128],
                        vslot(qt + r)[:, h * 128:(h + 1) * 128],
                        pbfset[ch][j][:, r * 128:(r + 1) * 128],
                        start=(r == 0), stop=(r == 4)),
                        sT if (j == 1 and r == 4) else None)
            av_done[g] = sT.n

        def c_evacOh(qt, m, ch):
            g = qt * NSUP + m
            SE.wait(sT, av_done[g])
            SE.do(lambda e, m=m, ch=ch: e.activation(
                out=attnfm_c[ch][:, m * 256:(m + 1) * 256], in_=psO_c[ch],
                func=ACT.Copy), sC)
            evacOh_done[g] = sC.n

        def c_wo(qt, ch):
            TE.wait(sC, evacOh_done[qt * NSUP + NSUP - 1])
            TE.wait(sG, wo_ld)
            VE.wait(sLx, x1_ld[qt])
            if qt > 0:
                VE.wait(sSp, x2_spill[qt - 1])
            for jq in range(4):
                if qt > 0 or jq > 0:
                    TE.wait(sV, sV.n)
                for cc in range(KC):
                    TE.do(lambda e, cc=cc, jq=jq, ch=ch: e.matmul(
                        psWq,
                        attnfm_c[ch][:, cc * 128:(cc + 1) * 128],
                        big0[:, cc * D + jq * 512:cc * D + (jq + 1) * 512],
                        start=(cc == 0), stop=(cc == KC - 1)),
                        sT if cc == KC - 1 else None)
                VE.wait(sT, sT.n)
                VE.do(lambda e, jq=jq, qt=qt: e.tensor_tensor(
                    out=x2b[:, jq * 512:(jq + 1) * 512],
                    in0=x1slot(qt)[:, jq * 512:(jq + 1) * 512],
                    in1=psWq, op=ALU.add), sV)
            GP.wait(sV, sV.n)
            GP.do(lambda e, qt=qt: e.dma_start(
                out=s_x2[qt * 128:(qt + 1) * 128, :], in_=x2b[:, :]), sSp, 16)
            x2_spill[qt] = sSp.n
            x2evac_done[qt] = sV.n

        for qp in range(TOWN // 2):
            qA, qB = 2 * qp, 2 * qp + 1
            for k in range(NSUP + 3):
                for ch, qt, kc in ((0, qA, k), (1, qB, k - 1)):
                    if 0 <= kc - 2 < NSUP:
                        c_evacOh(qt, kc - 2, ch)
                    if 0 <= kc - 1 < NSUP:
                        c_ptav(qt, kc - 1, ch)
                        if kc - 1 == NSUP - 1 and qp + 1 < TOWN // 2:
                            bt = qt + 6
                            SY.wait(sT, av_done[qt * NSUP + NSUP - 1])
                            SY.do(lambda e, bt=bt: e.dma_start(
                                out=vslot(bt),
                                in_=s_v[bt * 128:(bt + 1) * 128, :].bitcast(BF16)),
                                sLr, 16)
                    if 0 <= kc < NSUP:
                        c_scores(qt, kc, ch)
                        if kc == NSUP - 1 and qp + 1 < TOWN // 2:
                            bt = qt + 6
                            SY.wait(sT, sc_last[qt])
                            SY.do(lambda e, bt=bt: e.dma_start(
                                out=kslot(bt),
                                in_=s_k[bt * 128:(bt + 1) * 128, :].bitcast(BF16)),
                                sLr, 16)
                            SY.do(lambda e, qn=qt + 2: e.dma_start(
                                out=qslot(qn),
                                in_=s_q[qn * 128:(qn + 1) * 128, :].bitcast(BF16)),
                                sLr, 16)
            c_wo(qA, 0)
            c_wo(qB, 1)
            if qp + 1 < TOWN // 2:
                for qn in (qA + 2, qB + 2):
                    if qn - 2 in x2evac_done:
                        SY.wait(sV, x2evac_done[qn - 2])
                    SY.wait(sSp, x1_spills[qn + 4])
                    SY.do(lambda e, qn=qn: e.dma_start(
                        out=x1slot(qn)[:, :], in_=s_x1[qn * 128:(qn + 1) * 128, :]), sLx, 16)
                    x1_ld[qn] = sLx.n
                ring_rd[qp + 1] = sLr.n

        SY.wait(sT, sT.n)
        VE.wait(sT, sT.n)
        SE.wait(sT, sT.n)
        SY.wait(sSp, wi_spill)
        SY.do(lambda e: e.dma_start(out=big0[:, :], in_=s_wi.bitcast(BF16)), sLw, 16)
        wi_reld = sLw.n
        TE.wait(sLw, wi_reld)
        out_stores = {}
        Ast.clear()

        def emit_D_tail(t):
            xs = xslot(t)
            emit_resid(t, xs, xs)
            GP.wait(sV, Ast[("x1", t)])
            GP.do(lambda e, t=t, xs=xs: e.dma_start(
                out=o_x[t * 128:(t + 1) * 128, :], in_=xs[:, :]), sSp, 16)
            out_stores[t] = sSp.n

        SY.wait(sV, sV.n)
        SY.wait(sSp, x2_spill[0])
        SY.do(lambda e: e.dma_start(out=xbuf[0][:, :], in_=s_x2[0:128, :]), sLx, 16)
        Ast[("xld", 0)] = sLx.n
        for t in range(TOWN):
            i = t % 2
            VE.wait(sLx, Ast[("xld", t)])
            SE.wait(sLx, Ast[("xld", t)])
            ffn_prep(t, i, xslot(t))
            if t >= 2:
                emit_evacffn(t - 2)
            if t >= 1:
                ffn_mm(t - 1, (t - 1) % 2)
            if t >= 2:
                emit_D_tail(t - 2)
            if t + 1 < TOWN:
                if t + 1 <= 2:
                    SY.wait(sV, sV.n)
                if t - 2 in out_stores:
                    SY.wait(sSp, out_stores[t - 2])
                SY.wait(sSp, x2_spill[t + 1])
                SY.do(lambda e, t1=t + 1: e.dma_start(
                    out=xslot(t1)[:, :], in_=s_x2[t1 * 128:(t1 + 1) * 128, :]), sLx, 16)
                Ast[("xld", t + 1)] = sLx.n
        emit_evacffn(TOWN - 2)
        ffn_mm(TOWN - 1, (TOWN - 1) % 2)
        emit_D_tail(TOWN - 2)
        emit_evacffn(TOWN - 1)
        emit_D_tail(TOWN - 1)

        GP.wait(sSp, sSp.n)

        with nc.Block() as block:
            @block.sync
            def _(eng):
                for f in E["sync"].ops:
                    f(eng)

            @block.gpsimd
            def _(eng):
                for f in E["gpsimd"].ops:
                    f(eng)

            @block.tensor
            def _(eng):
                for f in E["tensor"].ops:
                    f(eng)

            @block.vector
            def _(eng):
                for f in E["vector"].ops:
                    f(eng)

            @block.scalar
            def _(eng):
                for f in E["scalar"].ops:
                    f(eng)

    return nc


def _host_inputs(L0_attn_x, L0_attn_Wq, L0_attn_Wk, L0_attn_Wv, L0_attn_Wo,
                 L0_ffn_gamma, L0_ffn_W):
    x = np.asarray(L0_attn_x, dtype=np.float32).reshape(SEQ, D)
    wqt = np.ascontiguousarray(np.asarray(L0_attn_Wq, dtype=np.float32).T)
    wkt = np.ascontiguousarray(np.asarray(L0_attn_Wk, dtype=np.float32).T)
    wvt = np.ascontiguousarray(np.asarray(L0_attn_Wv, dtype=np.float32).T)
    wot = np.ascontiguousarray(np.asarray(L0_attn_Wo, dtype=np.float32).T)
    wft = np.ascontiguousarray(np.asarray(L0_ffn_W, dtype=np.float32).T)
    gam = np.tile(np.asarray(L0_ffn_gamma, dtype=np.float32).reshape(1, D), (128, 1))
    ident = np.eye(128, dtype=ml_dtypes.bfloat16)
    ones = np.ones((128, 128), dtype=np.float32)
    tcst = np.tile(np.array([[EPS_Q, EPS_RMS * EPS_LN, 2.0, EPS_LN]],
                            dtype=np.float32), (128, 1))

    inv_freq = (10000.0 ** (-(np.arange(0, HD, 2, dtype=np.float64) / HD))).astype(np.float64)

    in_maps = []
    for c in range(NCORES):
        xhc = np.zeros((ROWS, D), dtype=np.float32)
        lo = c * OWN - HALO
        if c > 0:
            xhc[:HALO] = x[lo:lo + HALO]
        xhc[HALO:] = x[c * OWN:(c + 1) * OWN]

        pos = np.arange(lo, lo + ROWS, dtype=np.float64)
        ang = pos[:, None] * inv_freq[None, :]
        cosv = np.cos(ang).astype(np.float32)
        sinv = np.sin(ang).astype(np.float32)
        if c == 0:
            cosv[:HALO] = 1.0
            sinv[:HALO] = 0.0
        s2 = np.empty((ROWS, HD), dtype=np.float32)
        s2[:, 0::2] = -sinv
        s2[:, 1::2] = sinv
        cos_sb = cosv.reshape(TIL, 128, 64).transpose(1, 0, 2).reshape(128, TIL * 64)
        s2_sb = s2.reshape(TIL, 128, HD).transpose(1, 0, 2).reshape(128, TIL * HD)

        m = np.full((4, 5, 128, 128), NEG, dtype=np.float32)
        a = np.arange(128)[:, None]
        b = np.arange(128)[None, :]
        for qt in range(4):
            for r in range(5):
                delta = (512 + qt * 128 + a) - ((qt + r) * 128 + b)
                jg = c * OWN - HALO + (qt + r) * 128 + b
                valid = (delta >= 0) & (delta <= 511) & (jg >= 0)
                m[qt, r] = np.where(valid, 0.0, NEG)
        mwin = np.where(a < b, 0.0, NEG).astype(np.float32)
        mcau = np.where(a >= b, 0.0, NEG).astype(np.float32)
        mask_sb = np.concatenate(
            [m.transpose(2, 0, 1, 3).reshape(128, 4 * 640), mwin, mcau],
            axis=1).astype(ml_dtypes.bfloat16)

        in_maps.append(dict(
            xh=xhc, wqt=wqt, wkt=wkt, wvt=wvt, wot=wot, wft=wft, gam=gam,
            cosd=cos_sb, s2d=s2_sb, maskd=mask_sb, identd=ident, onesd=ones,
            tcstd=tcst))
    return in_maps


def kernel(L0_attn_x, L0_attn_Wq, L0_attn_Wk, L0_attn_Wv, L0_attn_Wo,
           L0_ffn_gamma, L0_ffn_W):
    in_maps = _host_inputs(L0_attn_x, L0_attn_Wq, L0_attn_Wk, L0_attn_Wv,
                           L0_attn_Wo, L0_ffn_gamma, L0_ffn_W)
    nc = build_nc(debug=_DEBUG)
    res = run_bass_kernel_spmd(nc, in_maps, core_ids=list(range(NCORES)))
    out = np.concatenate([res.results[c]["o_x"] for c in range(NCORES)], axis=0)
    if _DEBUG:
        kernel._last_results = res
    return out.reshape(1, SEQ, D)


# version 18
# speedup vs baseline: 1.0541x; 1.0435x over previous
import math
import numpy as np
import ml_dtypes
import concourse.bass as bass
import concourse.mybir as mybir
import concourse.bass_utils as _bu
from concourse.bass_utils import run_bass_kernel_spmd

if not getattr(_bu, "_ldw_patched", False):
    _orig_run_command = _bu.run_command

    def _run_command_ldw(cmd, *a, **k):
        cmd = [c.replace("--enable-ldw-opt=false", "--enable-ldw-opt=true")
               if isinstance(c, str) else c for c in cmd]
        return _orig_run_command(cmd, *a, **k)

    _bu.run_command = _run_command_ldw
    _bu._ldw_patched = True

F32 = mybir.dt.float32
BF16 = mybir.dt.bfloat16
AX = mybir.AxisListType.X
ALU = mybir.AluOpType
ACT = mybir.ActivationFunctionType

NCORES = 8
SEQ = 8192
D = 2048
NH = 16
HD = 128
KC = D // 128
OWN = SEQ // NCORES
HALO = 512
ROWS = OWN + HALO
TIL = ROWS // 128
TOWN = OWN // 128
CMAGIC = 12582912.0
SCALE = float(HD) ** -0.5
EPS_LN = 1e-5
EPS_RMS = 1e-6
EPS_Q = 1e-5
NEG = -1.0e30

_DEBUG = False


class Ctr:
    __slots__ = ("h", "n")

    def __init__(self, h):
        self.h = h
        self.n = 0


class Eng:

    def __init__(self, name):
        self.name = name
        self.ops = []
        self._waited = {}

    def wait(self, ctr, val=None):
        v = ctr.n if val is None else val
        if v <= 0 or self._waited.get(id(ctr), -1) >= v:
            return
        self._waited[id(ctr)] = v
        self.ops.append(lambda e, h=ctr.h, vv=v: e.wait_ge(h, vv))

    def do(self, fn, inc=None, k=1):
        if inc is None:
            self.ops.append(fn)
        else:
            inc.n += k
            self.ops.append(lambda e, f=fn, h=inc.h, kk=k: f(e).then_inc(h, kk))


def build_nc(debug=False):
    nc = bass.Bass("TRN2", target_bir_lowering=False)

    EI = dict(kind="ExternalInput")
    xh = nc.dram_tensor("xh", [ROWS, D], F32, **EI).ap()
    wqt = nc.dram_tensor("wqt", [D, D], F32, **EI).ap()
    wkt = nc.dram_tensor("wkt", [D, D], F32, **EI).ap()
    wvt = nc.dram_tensor("wvt", [D, D], F32, **EI).ap()
    wot = nc.dram_tensor("wot", [D, D], F32, **EI).ap()
    wft = nc.dram_tensor("wft", [D, D], F32, **EI).ap()
    gam = nc.dram_tensor("gam", [128, D], F32, **EI).ap()
    cosd = nc.dram_tensor("cosd", [128, TIL * 64], F32, **EI).ap()
    s2d = nc.dram_tensor("s2d", [128, TIL * 128], F32, **EI).ap()
    maskd = nc.dram_tensor("maskd", [128, 4 * 640 + 256], BF16, **EI).ap()
    identd = nc.dram_tensor("identd", [128, 128], BF16, **EI).ap()
    onesd = nc.dram_tensor("onesd", [128, 128], F32, **EI).ap()
    tcstd = nc.dram_tensor("tcstd", [128, 5], F32, **EI).ap()

    SC = dict(kind="ExternalOutput") if debug else {}
    s_wi = nc.dram_tensor("s_wi", [128, KC * D], mybir.dt.uint16, **SC).ap()
    s_x1 = nc.dram_tensor("s_x1", [OWN, D], F32, **SC).ap()
    s_x2 = nc.dram_tensor("s_x2", [OWN, D], F32, **SC).ap()
    U16 = mybir.dt.uint16
    s_k = nc.dram_tensor("s_k", [ROWS, D], U16, **SC).ap()
    s_v = nc.dram_tensor("s_v", [ROWS, D], U16, **SC).ap()
    s_q = nc.dram_tensor("s_q", [OWN, D], U16, **SC).ap()
    o_x = nc.dram_tensor("o_x", [OWN, D], F32, kind="ExternalOutput").ap()
    d_dbg = nc.dram_tensor("d_dbg", [128, 24], F32, kind="ExternalOutput").ap()

    from contextlib import ExitStack
    with ExitStack() as ctx:
        def sb(name, shape, dt):
            return ctx.enter_context(nc.sbuf_tensor(name, shape, dt))

        big0 = sb("big0", [128, KC * D], BF16)
        big1 = sb("big1", [128, TIL * D], BF16)
        gam_s = sb("gam_s", [128, D], F32)
        cos_s = sb("cos_s", [128, TIL * 64], F32)
        s2_s = sb("s2_s", [128, TIL * 128], F32)
        mask_s = sb("mask_s", [128, 4 * 640 + 256], BF16)
        ident = sb("ident", [128, 128], BF16)
        ones = sb("ones", [128, 128], F32)
        tcst = sb("tcst", [128, 5], F32)
        xbuf = [sb("xbuf0", [128, D], F32), sb("xbuf1", [128, D], F32)]
        x1b = sb("x1b", [128, D], F32)
        sA = sb("sA", [128, D], F32)
        sB = sb("sB", [128, D], F32)
        xi = [sb("xi0", [128, D], BF16), sb("xi1", [128, D], BF16)]
        xiT = sb("xiT", [128, D], BF16)
        xn2tmp = sb("xn2tmp", [128, D], BF16)
        krot = [sb("krot0", [128, D // 2], BF16), sb("krot1", [128, D // 2], BF16)]
        kfm = [sb("kfm0", [128, D // 2], BF16), sb("kfm1", [128, D // 2], BF16)]
        pbf = [sb("pbf0", [128, 640], BF16), sb("pbf1", [128, 640], BF16)]
        diagb = [sb("diagb0", [128, 128], BF16), sb("diagb1", [128, 128], BF16)]

        def tiny(name):
            return sb(name, [128, 1], F32)

        wparts = sb("wparts", [128, KC], F32)
        t_wsum = tiny("t_wsum")
        t_wtot = tiny("t_wtot")
        t_tw = tiny("t_tw")
        t_tw2 = tiny("t_tw2")
        t_wl = tiny("t_wl")
        t_wr0 = tiny("t_wr0")
        t_wu = tiny("t_wu")
        t_wu2 = tiny("t_wu2")
        t_ws = tiny("t_ws")
        def tpair(name):
            return [tiny(name + "0"), tiny(name + "1")]

        FT = {n: tpair("f_" + n) for n in
              ("sum", "ssq", "m", "v1", "mm", "var", "arg", "ln", "inv", "b",
               "nb", "am", "t", "l2", "r0", "u", "u2", "r1", "a", "s1")}
        FT["rs"] = [tiny("f_rs0"), tiny("f_rs1"), tiny("f_rs2")]
        UT = {n: tpair("u_" + n) for n in
              ("sum", "ssq", "m", "v1", "mm", "var", "arg", "ln", "inv", "b", "nb")}
        t_s1 = tiny("t_s1")
        t_s2 = tiny("t_s2")
        C_nm = [tiny("C_nm%d" % i) for i in range(4)]
        C_rs = [tiny("C_rs%d" % i) for i in range(4)]
        C_rl = [tiny("C_rl%d" % i) for i in range(4)]
        C_ri = [tiny("C_ri%d" % i) for i in range(4)]

        psA = ctx.enter_context(nc.psum_tensor("psA", [128, D], F32))
        psB = ctx.enter_context(nc.psum_tensor("psB", [128, D], F32))

        sLx = Ctr(ctx.enter_context(nc.semaphore("sLx")))
        sLw = Ctr(ctx.enter_context(nc.semaphore("sLw")))
        sLr = Ctr(ctx.enter_context(nc.semaphore("sLr")))
        sG = Ctr(ctx.enter_context(nc.semaphore("sG")))
        sSp = Ctr(ctx.enter_context(nc.semaphore("sSp")))
        sT = Ctr(ctx.enter_context(nc.semaphore("sT")))
        sV = Ctr(ctx.enter_context(nc.semaphore("sV")))
        sC = Ctr(ctx.enter_context(nc.semaphore("sC")))

        E = {n: Eng(n) for n in ("sync", "gpsimd", "tensor", "vector", "scalar")}
        SY, GP, TE, VE, SE = (E[n] for n in ("sync", "gpsimd", "tensor", "vector", "scalar"))

        for dst, src in ((gam_s, gam), (cos_s, cosd), (s2_s, s2d),
                         (mask_s, maskd), (ident, identd), (ones, onesd),
                         (tcst, tcstd)):
            SY.do(lambda e, d=dst, s=src: e.dma_start(out=d[:, :], in_=s), sLw, 16)
        consts_ld = sLw.n

        big0f = big0[:, :].bitcast(F32)
        big1f = big1[:, :].bitcast(F32)

        def stage_ap(c):
            if c < 8:
                return big0f[:, c * D:(c + 1) * D]
            if c < 14:
                return big1f[:, (c - 8) * D:(c - 7) * D]
            return xbuf[c - 14][:, :]

        def dma_rows(e, dst, c0, c1):
            src = xh
            return None

        SY.do(lambda e: e.dma_start(
            out=big0f[:, 0:8 * D].rearrange("p (c j) -> p c j", c=8),
            in_=wft[0:1024, :].rearrange("(c p) j -> p c j", c=8)), sLw, 16)
        SY.do(lambda e: e.dma_start(
            out=big1f[:, 0:6 * D].rearrange("p (c j) -> p c j", c=6),
            in_=wft[1024:1792, :].rearrange("(c p) j -> p c j", c=6)), sLw, 16)
        SY.do(lambda e: e.dma_start(out=xbuf[0][:, :], in_=wft[1792:1920, :]), sLw, 16)
        SY.do(lambda e: e.dma_start(out=xbuf[1][:, :], in_=wft[1920:2048, :]), sLw, 16)
        wst_ld = sLw.n

        SE.wait(sLw, wst_ld)
        for c in range(KC):
            SE.do(lambda e, c=c: e.activation(
                out=xi[0][:, :], in_=stage_ap(c), func=ACT.Abs,
                accum_out=wparts[:, c:c + 1]), sC)
        VE.wait(sC, sC.n)
        VE.do(lambda e: e.reduce_sum(t_wsum[:, :], wparts[:, :], axis=AX), sV)
        TE.wait(sV, sV.n)
        TE.wait(sLw, consts_ld)
        TE.do(lambda e: e.matmul(psA[:, 0:1], ones[:, :], t_wsum[:, :],
                                 start=True, stop=True), sT)
        SE.wait(sT, sT.n)
        SE.wait(sLw, consts_ld)
        SE.do(lambda e: e.activation(out=t_wtot[:, :], in_=psA[:, 0:1], func=ACT.Identity))
        SE.do(lambda e: e.activation(out=t_tw[:, :], in_=t_wtot[:, :], func=ACT.Identity,
                                     scale=1.0 / (D * D), bias=tcst[:, 0:1]))
        SE.do(lambda e: e.activation(out=t_tw2[:, :], in_=t_tw[:, :], func=ACT.Identity,
                                     scale=0.5 / 127.0))
        SE.do(lambda e: e.activation(out=t_wl[:, :], in_=t_tw[:, :], func=ACT.Ln))
        SE.do(lambda e: e.activation(out=t_wr0[:, :], in_=t_wl[:, :],
                                     func=ACT.Exp, scale=-1.0))
        SE.do(lambda e: e.activation(out=t_wu[:, :], in_=t_tw[:, :], func=ACT.Identity,
                                     scale=t_wr0[:, :]))
        SE.do(lambda e: e.activation(out=t_wu2[:, :], in_=t_wu[:, :], func=ACT.Identity,
                                     scale=-1.0, bias=tcst[:, 2:3]))
        SE.do(lambda e: e.activation(out=t_ws[:, :], in_=t_wu2[:, :], func=ACT.Identity,
                                     scale=t_wr0[:, :]), sC)
        VE.wait(sC, sC.n)
        for c in range(KC):
            VE.do(lambda e, c=c: e.tensor_scalar(sA[:, :], stage_ap(c), t_ws[:, :],
                                                 CMAGIC, ALU.mult, ALU.add))
            VE.do(lambda e: e.tensor_scalar(sB[:, :], sA[:, :], CMAGIC, None,
                                            ALU.subtract))
            VE.do(lambda e, c=c: e.tensor_scalar(big0[:, c * D:(c + 1) * D], sB[:, :],
                                                 1.0, -1.0, ALU.min, ALU.max), sV)
        quant_done_v = sV.n
        GP.wait(sV, quant_done_v)
        GP.do(lambda e: e.dma_start(out=s_wi.bitcast(BF16), in_=big0[:, :]), sSp, 16)
        wi_spill = sSp.n
        GP.do(lambda e: e.dma_start(out=d_dbg[:, 0:16], in_=wparts[:, :]), sSp, 16)
        for ii, tb in enumerate((t_wsum, t_wtot, t_tw, t_ws, t_wr0, t_wu, t_wu2)):
            GP.do(lambda e, ii=ii, tb=tb: e.dma_start(
                out=d_dbg[:, 16 + ii:17 + ii], in_=tb[:, :],
                allow_slow_non_contiguous=True), sSp, 16)

        Ast = {}

        def ffn_prep(t, i, xsrc):
            p = t % 2
            VE.do(lambda e: e.reduce_sum(FT["sum"][p][:, :], xsrc[:, :], axis=AX), sV)
            SE.wait(sV, sV.n)
            if ("tr", t - 2) in Ast:
                SE.wait(sT, Ast[("tr", t - 2)])
            SE.do(lambda e: e.activation(out=xi[i][:, :], in_=xsrc[:, :],
                                         func=ACT.Square, accum_out=FT["ssq"][p][:, :]))
            SE.do(lambda e: e.activation(out=FT["m"][p][:, :], in_=FT["sum"][p][:, :],
                                         func=ACT.Identity, scale=1.0 / D), sC)
            VE.wait(sC, sC.n)
            VE.do(lambda e: e.scalar_tensor_tensor(
                out=sB[:, :], in0=xsrc[:, :], scalar=FT["m"][p][:, :],
                in1=gam_s[:, :], op0=ALU.subtract, op1=ALU.mult))
            VE.do(lambda e: e.reduce_max(FT["am"][p][:, :], sB[:, :], axis=AX,
                                         apply_absolute_value=True), sV)
            SE.wait(sV, sV.n)
            SE.do(lambda e: e.activation(out=FT["v1"][p][:, :], in_=FT["ssq"][p][:, :],
                                         func=ACT.Identity, scale=1.0 / D))
            SE.do(lambda e: e.activation(out=FT["mm"][p][:, :], in_=FT["m"][p][:, :],
                                         func=ACT.Identity, scale=FT["m"][p][:, :]))
            SE.do(lambda e: e.activation(out=FT["var"][p][:, :], in_=FT["mm"][p][:, :],
                                         func=ACT.Identity, scale=-1.0, bias=FT["v1"][p][:, :]))
            SE.do(lambda e: e.activation(out=FT["arg"][p][:, :], in_=FT["var"][p][:, :],
                                         func=ACT.Identity, scale=1.0 + EPS_RMS,
                                         bias=tcst[:, 1:2]))
            SE.do(lambda e: e.activation(out=FT["ln"][p][:, :], in_=FT["arg"][p][:, :], func=ACT.Ln))
            SE.do(lambda e: e.activation(out=FT["inv"][p][:, :], in_=FT["ln"][p][:, :],
                                         func=ACT.Exp, scale=-0.5))
            SE.do(lambda e: e.activation(out=FT["t"][p][:, :], in_=FT["am"][p][:, :],
                                         func=ACT.Identity, scale=FT["inv"][p][:, :],
                                         bias=tcst[:, 0:1]))
            SE.do(lambda e: e.activation(out=FT["l2"][p][:, :], in_=FT["t"][p][:, :], func=ACT.Ln))
            SE.do(lambda e: e.activation(out=FT["r0"][p][:, :], in_=FT["l2"][p][:, :],
                                         func=ACT.Exp, scale=-1.0))
            SE.do(lambda e: e.activation(out=FT["u"][p][:, :], in_=FT["ln"][p][:, :],
                                         func=ACT.Exp, scale=-0.5, bias=tcst[:, 4:5]))
            SE.do(lambda e: e.activation(out=FT["u2"][p][:, :], in_=FT["t"][p][:, :],
                                         func=ACT.Identity, scale=FT["r0"][p][:, :]))
            SE.do(lambda e: e.activation(out=FT["b"][p][:, :], in_=FT["u2"][p][:, :],
                                         func=ACT.Identity, scale=-1.0, bias=tcst[:, 2:3]))
            SE.do(lambda e: e.activation(out=FT["r1"][p][:, :], in_=FT["b"][p][:, :],
                                         func=ACT.Identity, scale=FT["r0"][p][:, :]))
            SE.do(lambda e: e.activation(out=FT["a"][p][:, :], in_=FT["r1"][p][:, :],
                                         func=ACT.Identity, scale=FT["u"][p][:, :]))
            SE.do(lambda e, t=t: e.activation(out=FT["rs"][t % 3][:, :], in_=FT["t"][p][:, :],
                                         func=ACT.Identity, scale=t_tw2[:, :]), sC)
            VE.wait(sC, sC.n)
            VE.do(lambda e: e.tensor_scalar(sA[:, :], sB[:, :], FT["a"][p][:, :],
                                            CMAGIC, ALU.mult, ALU.add))
            if ("tr", t - 2) in Ast:
                VE.wait(sT, Ast[("tr", t - 2)])
            VE.do(lambda e, i=i: e.tensor_scalar(xi[i][:, :], sA[:, :], CMAGIC, None,
                                                 ALU.subtract), sV)
            Ast[("quant", t)] = sV.n

        def ffnb(t):
            return (krot[0], krot[1]) if t % 2 == 0 else (kfm[0], kfm[1])

        def ffn_mm(t, i):
            p = t % 2
            TE.wait(sV, Ast[("quant", t)])
            if ("evacxn2", t - 2) in Ast:
                TE.wait(sC, Ast[("evacxn2", t - 2)])
            for c in range(KC):
                TE.do(lambda e, c=c, i=i: e.matmul(
                    psA[:, c * 128:(c + 1) * 128], xi[i][:, c * 128:(c + 1) * 128],
                    ident[:, :], start=True, stop=True), sT if c == KC - 1 else None)
            Ast[("tr", t)] = sT.n
            SE.wait(sT, sT.n)
            SE.do(lambda e: e.activation(out=xiT[:, :], in_=psA[:, :], func=ACT.Copy), sC)
            Ast[("evacxiT", t)] = sC.n
            TE.wait(sC, sC.n)
            if ("evacffn", t - 1) in Ast:
                TE.wait(sC, Ast[("evacffn", t - 1)])
            for c in range(KC):
                for ss in range(4):
                    TE.do(lambda e, c=c, ss=ss: e.matmul(
                        psB[:, ss * 512:(ss + 1) * 512],
                        xiT[:, c * 128:(c + 1) * 128],
                        big0[:, c * D + ss * 512: c * D + (ss + 1) * 512],
                        start=(c == 0), stop=(c == KC - 1)),
                        sT if (c == KC - 1 and ss == 3) else None)
            Ast[("mm", t)] = sT.n

        def emit_evacffn(t):
            p = t % 2
            fb = ffnb(t)
            SE.wait(sT, Ast[("mm", t)])
            if ("x1", t - 2) in Ast:
                SE.wait(sV, Ast[("x1", t - 2)])
            SE.do(lambda e, fb=fb, t=t: e.activation(
                out=fb[0][:, :], in_=psB[:, 0:1024], func=ACT.Identity,
                scale=FT["rs"][t % 3][:, :]))
            SE.do(lambda e, fb=fb, t=t: e.activation(
                out=fb[1][:, :], in_=psB[:, 1024:2048], func=ACT.Identity,
                scale=FT["rs"][t % 3][:, :]), sC)
            Ast[("evacffn", t)] = sC.n

        def emit_resid(t, xsrc, xdst):
            fb = ffnb(t)
            VE.wait(sC, Ast[("evacffn", t)])
            VE.do(lambda e, fb=fb, xsrc=xsrc, xdst=xdst: e.tensor_tensor(
                out=xdst[:, 0:1024], in0=xsrc[:, 0:1024], in1=fb[0][:, :], op=ALU.add))
            VE.do(lambda e, fb=fb, xsrc=xsrc, xdst=xdst: e.tensor_tensor(
                out=xdst[:, 1024:2048], in0=xsrc[:, 1024:2048], in1=fb[1][:, :],
                op=ALU.add), sV)
            Ast[("x1", t)] = sV.n

        x1_spills = {}

        def xslot(t):
            return (xbuf[0], xbuf[1], x1b)[t % 3]

        def emit_xn2tr(t):
            TE.wait(sV, Ast[("xn2", t)])
            key = ("evacxiT", t + 2) if ("evacxiT", t + 2) in Ast else (
                ("evacxiT", t + 1) if ("evacxiT", t + 1) in Ast else ("evacxiT", t))
            TE.wait(sC, Ast[key])
            for c in range(KC):
                TE.do(lambda e, c=c: e.matmul(
                    psA[:, c * 128:(c + 1) * 128], xn2tmp[:, c * 128:(c + 1) * 128],
                    ident[:, :], start=True, stop=True), sT if c == KC - 1 else None)
            Ast[("xn2tr", t)] = sT.n
            SE.wait(sT, sT.n)
            SE.do(lambda e, t=t: e.activation(out=big1[:, t * D:(t + 1) * D],
                                              in_=psA[:, :], func=ACT.Copy), sC)
            Ast[("evacxn2", t)] = sC.n

        def emit_A_tail(t):
            p = t % 2
            xs = xslot(t)
            emit_resid(t, xs, xs)
            if t >= 4:
                GP.wait(sV, Ast[("x1", t)])
                GP.do(lambda e, t=t, xs=xs: e.dma_start(
                    out=s_x1[(t - 4) * 128:(t - 3) * 128, :], in_=xs[:, :]), sSp, 16)
                x1_spills[t] = sSp.n
            VE.do(lambda e, xs=xs: e.reduce_sum(UT["sum"][p][:, :], xs[:, :],
                                                axis=AX), sV)
            SE.wait(sV, sV.n)
            if ("xn2tr", t - 1) in Ast:
                SE.wait(sT, Ast[("xn2tr", t - 1)])
            SE.do(lambda e, xs=xs: e.activation(out=xn2tmp[:, :], in_=xs[:, :],
                                                func=ACT.Square, accum_out=UT["ssq"][p][:, :]))
            SE.do(lambda e: e.activation(out=UT["m"][p][:, :], in_=UT["sum"][p][:, :],
                                         func=ACT.Identity, scale=1.0 / D))
            SE.do(lambda e: e.activation(out=UT["v1"][p][:, :], in_=UT["ssq"][p][:, :],
                                         func=ACT.Identity, scale=1.0 / D))
            SE.do(lambda e: e.activation(out=UT["mm"][p][:, :], in_=UT["m"][p][:, :],
                                         func=ACT.Identity, scale=UT["m"][p][:, :]))
            SE.do(lambda e: e.activation(out=UT["var"][p][:, :], in_=UT["mm"][p][:, :],
                                         func=ACT.Identity, scale=-1.0, bias=UT["v1"][p][:, :]))
            SE.do(lambda e: e.activation(out=UT["arg"][p][:, :], in_=UT["var"][p][:, :],
                                         func=ACT.Identity, bias=tcst[:, 3:4]))
            SE.do(lambda e: e.activation(out=UT["ln"][p][:, :], in_=UT["arg"][p][:, :], func=ACT.Ln))
            SE.do(lambda e: e.activation(out=UT["inv"][p][:, :], in_=UT["ln"][p][:, :],
                                         func=ACT.Exp, scale=-0.5))
            SE.do(lambda e: e.activation(out=UT["b"][p][:, :], in_=UT["m"][p][:, :],
                                         func=ACT.Identity, scale=UT["inv"][p][:, :]))
            SE.do(lambda e: e.activation(out=UT["nb"][p][:, :], in_=UT["b"][p][:, :],
                                         func=ACT.Identity, scale=-1.0), sC)
            VE.wait(sC, sC.n)
            if ("xn2tr", t - 1) in Ast:
                VE.wait(sT, Ast[("xn2tr", t - 1)])
            VE.do(lambda e, xs=xs: e.tensor_scalar(xn2tmp[:, :], xs[:, :],
                                                   UT["inv"][p][:, :], UT["nb"][p][:, :],
                                                   ALU.mult, ALU.add), sV)
            Ast[("xn2", t)] = sV.n

        SY.wait(sV, quant_done_v)
        SY.do(lambda e: e.dma_start(out=xbuf[0][:, :], in_=xh[0:128, :]), sLx, 16)
        Ast[("xld", 0)] = sLx.n
        for t in range(TIL):
            i = t % 2
            VE.wait(sLx, Ast[("xld", t)])
            SE.wait(sLx, Ast[("xld", t)])
            ffn_prep(t, i, xslot(t))
            if t >= 2:
                emit_evacffn(t - 2)
            if t >= 1:
                ffn_mm(t - 1, (t - 1) % 2)
            if t >= 2:
                emit_A_tail(t - 2)
                emit_xn2tr(t - 2)
            if t + 1 < TIL:
                if t + 1 <= 2:
                    SY.wait(sV, quant_done_v)
                if ("x1", t - 2) in Ast:
                    SY.wait(sV, Ast[("xn2", t - 2)])
                    if t - 2 >= 4:
                        SY.wait(sSp, x1_spills[t - 2])
                SY.do(lambda e, t1=t + 1: e.dma_start(
                    out=xslot(t1)[:, :], in_=xh[t1 * 128:(t1 + 1) * 128, :]), sLx, 16)
                Ast[("xld", t + 1)] = sLx.n
        emit_evacffn(TIL - 2)
        ffn_mm(TIL - 1, (TIL - 1) % 2)
        emit_A_tail(TIL - 2)
        emit_xn2tr(TIL - 2)
        emit_evacffn(TIL - 1)
        emit_A_tail(TIL - 1)
        emit_xn2tr(TIL - 1)

        HW = KC * D // 2
        subs = []
        for wsrc, kind in ((wkt, "k"), (wvt, "v"), (wqt, "q")):
            for hf in range(2):
                subs.append((wsrc, kind, hf))
        sub_done_T = [0] * len(subs)
        TE.wait(sC, sC.n)

        def emit_rope_or_evac(kind, t, tp, hf):
            po = tp * 1024
            if kind == "v":
                SE.wait(sT, Bst["mm"][t])
                if t - 2 in Bst["spill"]:
                    SE.wait(sSp, Bst["spill"][t - 2])
                SE.do(lambda e, tp=tp: e.activation(
                    out=krot[tp][:, :], in_=psA[:, tp * 1024:(tp + 1) * 1024],
                    func=ACT.Copy), sC)
                Bst["rope"][t] = ("C", sC.n)
            else:
                VE.wait(sT, Bst["mm"][t])
                if t - 2 in Bst["tr"]:
                    VE.wait(sT, Bst["tr"][t - 2])
                c2 = bass.AP(cos_s, t * 64,
                             [[TIL * 64, 128], [0, 8], [1, 64], [0, 2]])
                s2 = bass.AP(s2_s, t * 128,
                             [[TIL * 128, 128], [0, 8], [1, 128]])
                qv = bass.AP(psA, po, [[D, 128], [128, 8], [2, 64], [1, 2]])
                qsw = bass.AP(psA, po + 1, [[D, 128], [128, 8], [2, 64], [-1, 2]])
                sAh = bass.AP(sA, 0, [[D, 128], [128, 8], [2, 64], [1, 2]])
                sBh = bass.AP(sB, 0, [[D, 128], [128, 8], [2, 64], [1, 2]])
                VE.do(lambda e, qv=qv, c2=c2, sAh=sAh: e.tensor_tensor(
                    out=sAh, in0=qv, in1=c2, op=ALU.mult))
                VE.do(lambda e, qsw=qsw, s2=s2, sBh=sBh: e.tensor_tensor(
                    out=sBh, in0=qsw, in1=s2, op=ALU.mult))
                VE.do(lambda e, tp=tp: e.tensor_tensor(
                    out=krot[tp][:, :], in0=sA[:, 0:1024], in1=sB[:, 0:1024],
                    op=ALU.add), sV)
                Bst["rope"][t] = ("V", sV.n)

        def emit_tr_evac_spill(kind, t, tp, hf):
            if kind == "v":
                eng, val = Bst["rope"][t]
                GP.wait(sC, val)
                GP.do(lambda e, t=t, hf=hf, tp=tp: e.dma_start(
                    out=s_v[t * 128:(t + 1) * 128,
                            hf * 1024:(hf + 1) * 1024].bitcast(BF16),
                    in_=krot[tp][:, :]), sSp, 16)
                Bst["spill"][t] = sSp.n
                return
            eng, val = Bst["rope"][t]
            TE.wait(sV, val)
            if t - 1 in Bst["evac"]:
                TE.wait(sC, Bst["evac"][t - 1])
            for hh in range(8):
                TE.do(lambda e, hh=hh, tp=tp: e.matmul(
                    psB[:, hh * 128:(hh + 1) * 128],
                    krot[tp][:, hh * 128:(hh + 1) * 128],
                    ident[:, :], start=True, stop=True),
                    sT if hh == 7 else None)
            Bst["tr"][t] = sT.n
            SE.wait(sT, Bst["tr"][t])
            if t - 2 in Bst["spill"]:
                SE.wait(sSp, Bst["spill"][t - 2])
            kscale = SCALE if kind == "k" else 1.0
            SE.do(lambda e, tp=tp, ks=kscale: e.activation(
                out=kfm[tp][:, :], in_=psB[:, 0:1024], func=ACT.Identity,
                scale=ks), sC)
            Bst["evac"][t] = sC.n
            GP.wait(sC, Bst["evac"][t])
            dst = s_k if kind == "k" else s_q
            toff = t if kind == "k" else t - 4
            GP.do(lambda e, dst=dst, toff=toff, hf=hf, tp=tp: e.dma_start(
                out=dst[toff * 128:(toff + 1) * 128,
                        hf * 1024:(hf + 1) * 1024].bitcast(BF16),
                in_=kfm[tp][:, :]), sSp, 16)
            Bst["spill"][t] = sSp.n

        def emit_wload(si):
            wsrc, kind, hf = subs[si]
            half = si % 2
            if si < 2:
                GP.wait(sT, sT.n)
            else:
                GP.wait(sT, sub_done_T[si - 2])
            GP.do(lambda e, w=wsrc, hf=hf, half=half: e.dma_start(
                out=big0[:, half * HW:(half + 1) * HW].rearrange("p (c j) -> p c j", c=KC),
                in_=w[:, hf * (D // 2):(hf + 1) * (D // 2)]
                    .rearrange("(c p) j -> p c j", c=KC)), sG, 16)
            return sG.n

        wload_done = [0] * len(subs)
        wload_done[0] = emit_wload(0)
        wload_done[1] = emit_wload(1)
        for si, (wsrc, kind, hf) in enumerate(subs):
            half = si % 2
            Bst = {"mm": {}, "rope": {}, "tr": {}, "evac": {}, "spill": {}}
            tiles = list(range(TIL)) if kind in ("k", "v") else list(range(4, TIL))
            TE.wait(sG, wload_done[si])
            TE.wait(sV, sV.n)
            TE.wait(sC, sC.n)
            VE.wait(sSp, sSp.n)
            SE.wait(sSp, sSp.n)
            prev = None
            for t in tiles:
                tp = t % 2
                if t - 2 in Bst["rope"]:
                    eng, val = Bst["rope"][t - 2]
                    if eng == "V":
                        TE.wait(sV, val)
                    else:
                        TE.wait(sC, val)
                for c in range(KC):
                    for ss in range(2):
                        TE.do(lambda e, c=c, ss=ss, t=t, half=half, tp=tp: e.matmul(
                            psA[:, tp * 1024 + ss * 512: tp * 1024 + (ss + 1) * 512],
                            big1[:, t * D + c * 128:t * D + (c + 1) * 128],
                            big0[:, half * HW + c * 1024 + ss * 512:
                                    half * HW + c * 1024 + (ss + 1) * 512],
                            start=(c == 0), stop=(c == KC - 1)),
                            sT if (c == KC - 1 and ss == 1) else None)
                Bst["mm"][t] = sT.n
                emit_rope_or_evac(kind, t, tp, hf)
                if prev is not None:
                    emit_tr_evac_spill(kind, prev, prev % 2, hf)
                prev = t
            emit_tr_evac_spill(kind, prev, prev % 2, hf)
            sub_done_T[si] = sT.n
            if si + 2 < len(subs):
                wload_done[si + 2] = emit_wload(si + 2)

        qsl = sA[:, :].bitcast(BF16)

        def kslot(bt):
            return big1[:, (bt % 6) * D:((bt % 6) + 1) * D]

        def vslot(bt):
            return big1[:, (6 + bt % 6) * D:(7 + bt % 6) * D]

        def qslot(qt):
            return qsl[:, (qt % 2) * D:((qt % 2) + 1) * D]

        def x1slot(qt):
            return (x1b if qt % 2 == 0 else sB)

        attnfm = xi[0]
        x2b = xbuf[0]

        GP.wait(sT, sT.n)
        GP.do(lambda e: e.dma_start(
            out=big0[:, :].rearrange("p (c j) -> p c j", c=KC),
            in_=wot.rearrange("(c p) j -> p c j", c=KC)), sG, 16)
        wo_ld = sG.n

        SY.wait(sT, sT.n)
        SY.wait(sV, sV.n)
        SY.wait(sSp, sSp.n)
        for bt in range(5):
            SY.do(lambda e, bt=bt: e.dma_start(
                out=kslot(bt), in_=s_k[bt * 128:(bt + 1) * 128, :].bitcast(BF16)), sLr, 16)
            SY.do(lambda e, bt=bt: e.dma_start(
                out=vslot(bt), in_=s_v[bt * 128:(bt + 1) * 128, :].bitcast(BF16)), sLr, 16)
        SY.do(lambda e: e.dma_start(out=qslot(0), in_=s_q[0:128, :].bitcast(BF16)), sLr, 16)
        SY.wait(sSp, x1_spills[4])
        SY.do(lambda e: e.dma_start(out=x1slot(0)[:, :], in_=s_x1[0:128, :]), sLx, 16)
        ring_ready = {0: sLr.n}
        x1_ld = {0: sLx.n}

        NSUP = NH // 2
        SLOT = [(psB, 0), (psA, 512)]

        def sblk(par, j, r=None, n=640):
            tns, base = SLOT[par]
            off = base + j * 640 + (0 if r is None else r * 128)
            ln = n if r is None else 128
            return tns[:, off:off + ln]

        def s2h_ap(par, off, blocklen):
            tns, base = SLOT[par]
            return bass.AP(tns, base + off, [[2048, 128], [640, 2], [1, blocklen]])

        psO = psA[:, 0:256]
        psW = psA[:, 1024:2048]
        pbfset = [(pbf[0], pbf[1]), (kfm[0][:, 0:640], kfm[1][:, 0:640])]
        diagset = [(diagb[0], krot[0][:, 0:128]), (diagb[1], krot[1][:, 0:128])]

        evacPT_done = [0, 0]
        sc_done = {}
        vmax_done = {}
        recip_done = {}
        diag_done = {}
        pt_done = {}
        av_done = {}
        x2_spill = {}
        sc_last = {}
        x2evac_done = {}
        for qt in range(TOWN):
            if qt + 1 < TOWN:
                bt = qt + 5
                if qt - 1 in sc_last:
                    SY.wait(sT, sc_last[qt - 1])
                SY.do(lambda e, bt=bt: e.dma_start(
                    out=kslot(bt), in_=s_k[bt * 128:(bt + 1) * 128, :].bitcast(BF16)), sLr, 16)
                SY.do(lambda e, q1=qt + 1: e.dma_start(
                    out=qslot(q1), in_=s_q[q1 * 128:(q1 + 1) * 128, :].bitcast(BF16)), sLr, 16)
                if qt - 1 in x2evac_done:
                    SY.wait(sV, x2evac_done[qt - 1])
                SY.wait(sSp, x1_spills[qt + 5])
                SY.do(lambda e, q1=qt + 1: e.dma_start(
                    out=x1slot(q1)[:, :], in_=s_x1[q1 * 128:(q1 + 1) * 128, :]), sLx, 16)
                x1_ld[qt + 1] = sLx.n
                gprev = (qt - 1) * NSUP + NSUP - 1
                if gprev in av_done:
                    SY.wait(sT, av_done[gprev])
                SY.do(lambda e, bt=bt: e.dma_start(
                    out=vslot(bt), in_=s_v[bt * 128:(bt + 1) * 128, :].bitcast(BF16)), sLr, 16)
                ring_ready[qt + 1] = sLr.n
            ring_ld = ring_ready[qt]

            for k in range(NSUP + 2):
                if k < NSUP:
                    m = k
                    g = qt * NSUP + m
                    par = m % 2
                    h0 = 2 * m
                    TE.wait(sLr, ring_ld)
                    if evacPT_done[par]:
                        TE.wait(sC, evacPT_done[par])
                    if par == 1 and m == 1 and qt > 0:
                        TE.wait(sV, x2evac_done[qt - 1])
                    for j in range(2):
                        for r in range(5):
                            TE.do(lambda e, h=h0 + j, r=r, qt=qt, par=par, j=j: e.matmul(
                                sblk(par, j, r),
                                qslot(qt)[:, h * 128:(h + 1) * 128],
                                kslot(qt + r)[:, h * 128:(h + 1) * 128],
                                start=True, stop=True),
                                sT if (j == 1 and r == 4) else None)
                    sc_done[g] = sT.n
                    if m == NSUP - 1:
                        sc_last[qt] = sT.n
                    VE.wait(sT, sc_done[g])
                    if qt < 4:
                        VE.do(lambda e, par=par, qt=qt: e.tensor_tensor(
                            out=s2h_ap(par, 0, 640), in0=s2h_ap(par, 0, 640),
                            in1=bass.AP(mask_s, qt * 640,
                                        [[2816, 128], [0, 2], [1, 640]]),
                            op=ALU.add))
                    else:
                        VE.do(lambda e, par=par: e.tensor_tensor(
                            out=s2h_ap(par, 0, 128), in0=s2h_ap(par, 0, 128),
                            in1=bass.AP(mask_s, 2560, [[2816, 128], [0, 2], [1, 128]]),
                            op=ALU.add))
                        VE.do(lambda e, par=par: e.tensor_tensor(
                            out=s2h_ap(par, 512, 128), in0=s2h_ap(par, 512, 128),
                            in1=bass.AP(mask_s, 2688, [[2816, 128], [0, 2], [1, 128]]),
                            op=ALU.add))
                    for j in range(2):
                        VE.do(lambda e, par=par, j=j: e.reduce_max(
                            C_nm[par * 2 + j][:, :], sblk(par, j), axis=AX,
                            negate=True), sV if j == 1 else None)
                    vmax_done[g] = sV.n
                    SE.wait(sV, vmax_done[g])
                    if g - 2 in av_done:
                        SE.wait(sT, av_done[g - 2])
                    for j in range(2):
                        SE.do(lambda e, par=par, j=j: e.activation(
                            out=pbfset[par][j][:, :], in_=sblk(par, j), func=ACT.Exp,
                            bias=C_nm[par * 2 + j][:, :], scale=1.0,
                            accum_out=C_rs[par * 2 + j][:, :]))
                    for j in range(2):
                        SE.do(lambda e, par=par, j=j: e.activation(
                            out=C_rl[par * 2 + j][:, :], in_=C_rs[par * 2 + j][:, :],
                            func=ACT.Ln))
                        SE.do(lambda e, par=par, j=j: e.activation(
                            out=C_ri[par * 2 + j][:, :], in_=C_rl[par * 2 + j][:, :],
                            func=ACT.Exp, scale=-1.0),
                            sC if j == 1 else None)
                    recip_done[g] = sC.n
                if 0 <= k - 2 < NSUP:
                    m = k - 2
                    g = qt * NSUP + m
                    SE.wait(sT, av_done[g])
                    SE.do(lambda e, m=m: e.activation(
                        out=attnfm[:, m * 256:(m + 1) * 256], in_=psO,
                        func=ACT.Copy), sC)
                if 0 <= k - 1 < NSUP:
                    m = k - 1
                    g = qt * NSUP + m
                    par = m % 2
                    VE.wait(sC, recip_done[g])
                    if g - 2 in pt_done:
                        VE.wait(sT, pt_done[g - 2])
                    for j in range(2):
                        VE.do(lambda e, par=par, j=j: e.tensor_scalar(
                            diagset[par][j][:, :], ident[:, :],
                            C_ri[par * 2 + j][:, :], None, ALU.mult),
                            sV if j == 1 else None)
                    diag_done[g] = sV.n
                    TE.wait(sV, diag_done[g])
                    for j in range(2):
                        for r in range(5):
                            TE.do(lambda e, r=r, par=par, j=j: e.matmul(
                                sblk(par, j, r),
                                pbfset[par][j][:, r * 128:(r + 1) * 128],
                                diagset[par][j][:, :], start=True, stop=True),
                                sT if (j == 1 and r == 4) else None)
                    pt_done[g] = sT.n
                    SE.wait(sT, pt_done[g])
                    for j in range(2):
                        SE.do(lambda e, par=par, j=j: e.activation(
                            out=pbfset[par][j][:, :], in_=sblk(par, j),
                            func=ACT.Copy), sC if j == 1 else None)
                    evacPT_done[par] = sC.n
                    TE.wait(sC, evacPT_done[par])
                    for j in range(2):
                        for r in range(5):
                            TE.do(lambda e, h=2 * m + j, r=r, qt=qt, par=par, j=j: e.matmul(
                                psO[:, j * 128:(j + 1) * 128],
                                vslot(qt + r)[:, h * 128:(h + 1) * 128],
                                pbfset[par][j][:, r * 128:(r + 1) * 128],
                                start=(r == 0), stop=(r == 4)),
                                sT if (j == 1 and r == 4) else None)
                    av_done[g] = sT.n
            TE.wait(sC, sC.n)
            TE.wait(sG, wo_ld)
            VE.wait(sLx, x1_ld[qt])
            if qt > 0:
                VE.wait(sSp, x2_spill[qt - 1])
            for jh in range(2):
                TE.wait(sV, sV.n)
                for c in range(KC):
                    for s in range(2):
                        TE.do(lambda e, c=c, s=s, jh=jh: e.matmul(
                            psW[:, s * 512:(s + 1) * 512],
                            attnfm[:, c * 128:(c + 1) * 128],
                            big0[:, c * D + jh * 1024 + s * 512:
                                    c * D + jh * 1024 + (s + 1) * 512],
                            start=(c == 0), stop=(c == KC - 1)),
                            sT if (c == KC - 1 and s == 1) else None)
                VE.wait(sT, sT.n)
                VE.do(lambda e, jh=jh, qt=qt: e.tensor_tensor(
                    out=x2b[:, jh * 1024:(jh + 1) * 1024],
                    in0=x1slot(qt)[:, jh * 1024:(jh + 1) * 1024],
                    in1=psW, op=ALU.add), sV)
            GP.wait(sV, sV.n)
            GP.do(lambda e, qt=qt: e.dma_start(
                out=s_x2[qt * 128:(qt + 1) * 128, :], in_=x2b[:, :]), sSp, 16)
            x2_spill[qt] = sSp.n
            x2evac_done[qt] = sV.n

        SY.wait(sT, sT.n)
        VE.wait(sT, sT.n)
        SE.wait(sT, sT.n)
        SY.wait(sSp, wi_spill)
        SY.do(lambda e: e.dma_start(out=big0[:, :], in_=s_wi.bitcast(BF16)), sLw, 16)
        wi_reld = sLw.n
        TE.wait(sLw, wi_reld)
        out_stores = {}
        Ast.clear()

        def emit_D_tail(t):
            xs = xslot(t)
            emit_resid(t, xs, xs)
            GP.wait(sV, Ast[("x1", t)])
            GP.do(lambda e, t=t, xs=xs: e.dma_start(
                out=o_x[t * 128:(t + 1) * 128, :], in_=xs[:, :]), sSp, 16)
            out_stores[t] = sSp.n

        SY.wait(sV, sV.n)
        SY.wait(sSp, x2_spill[0])
        SY.do(lambda e: e.dma_start(out=xbuf[0][:, :], in_=s_x2[0:128, :]), sLx, 16)
        Ast[("xld", 0)] = sLx.n
        for t in range(TOWN):
            i = t % 2
            VE.wait(sLx, Ast[("xld", t)])
            SE.wait(sLx, Ast[("xld", t)])
            ffn_prep(t, i, xslot(t))
            if t >= 2:
                emit_evacffn(t - 2)
            if t >= 1:
                ffn_mm(t - 1, (t - 1) % 2)
            if t >= 2:
                emit_D_tail(t - 2)
            if t + 1 < TOWN:
                if t + 1 <= 2:
                    SY.wait(sV, sV.n)
                if t - 2 in out_stores:
                    SY.wait(sSp, out_stores[t - 2])
                SY.wait(sSp, x2_spill[t + 1])
                SY.do(lambda e, t1=t + 1: e.dma_start(
                    out=xslot(t1)[:, :], in_=s_x2[t1 * 128:(t1 + 1) * 128, :]), sLx, 16)
                Ast[("xld", t + 1)] = sLx.n
        emit_evacffn(TOWN - 2)
        ffn_mm(TOWN - 1, (TOWN - 1) % 2)
        emit_D_tail(TOWN - 2)
        emit_evacffn(TOWN - 1)
        emit_D_tail(TOWN - 1)

        GP.wait(sSp, sSp.n)

        with nc.Block() as block:
            @block.sync
            def _(eng):
                for f in E["sync"].ops:
                    f(eng)

            @block.gpsimd
            def _(eng):
                for f in E["gpsimd"].ops:
                    f(eng)

            @block.tensor
            def _(eng):
                for f in E["tensor"].ops:
                    f(eng)

            @block.vector
            def _(eng):
                for f in E["vector"].ops:
                    f(eng)

            @block.scalar
            def _(eng):
                for f in E["scalar"].ops:
                    f(eng)

    return nc


def _host_inputs(L0_attn_x, L0_attn_Wq, L0_attn_Wk, L0_attn_Wv, L0_attn_Wo,
                 L0_ffn_gamma, L0_ffn_W):
    x = np.asarray(L0_attn_x, dtype=np.float32).reshape(SEQ, D)
    wqt = np.ascontiguousarray(np.asarray(L0_attn_Wq, dtype=np.float32).T)
    wkt = np.ascontiguousarray(np.asarray(L0_attn_Wk, dtype=np.float32).T)
    wvt = np.ascontiguousarray(np.asarray(L0_attn_Wv, dtype=np.float32).T)
    wot = np.ascontiguousarray(np.asarray(L0_attn_Wo, dtype=np.float32).T)
    wft = np.ascontiguousarray(np.asarray(L0_ffn_W, dtype=np.float32).T)
    gam = np.tile(np.asarray(L0_ffn_gamma, dtype=np.float32).reshape(1, D), (128, 1))
    ident = np.eye(128, dtype=ml_dtypes.bfloat16)
    ones = np.ones((128, 128), dtype=np.float32)
    tcst = np.tile(np.array([[EPS_Q, EPS_RMS * EPS_LN, 2.0, EPS_LN,
                              math.log(127.0)]],
                            dtype=np.float32), (128, 1))

    inv_freq = (10000.0 ** (-(np.arange(0, HD, 2, dtype=np.float64) / HD))).astype(np.float64)

    in_maps = []
    for c in range(NCORES):
        xhc = np.zeros((ROWS, D), dtype=np.float32)
        lo = c * OWN - HALO
        if c > 0:
            xhc[:HALO] = x[lo:lo + HALO]
        xhc[HALO:] = x[c * OWN:(c + 1) * OWN]

        pos = np.arange(lo, lo + ROWS, dtype=np.float64)
        ang = pos[:, None] * inv_freq[None, :]
        cosv = np.cos(ang).astype(np.float32)
        sinv = np.sin(ang).astype(np.float32)
        if c == 0:
            cosv[:HALO] = 1.0
            sinv[:HALO] = 0.0
        s2 = np.empty((ROWS, HD), dtype=np.float32)
        s2[:, 0::2] = -sinv
        s2[:, 1::2] = sinv
        cos_sb = cosv.reshape(TIL, 128, 64).transpose(1, 0, 2).reshape(128, TIL * 64)
        s2_sb = s2.reshape(TIL, 128, HD).transpose(1, 0, 2).reshape(128, TIL * HD)

        m = np.full((4, 5, 128, 128), NEG, dtype=np.float32)
        a = np.arange(128)[:, None]
        b = np.arange(128)[None, :]
        for qt in range(4):
            for r in range(5):
                delta = (512 + qt * 128 + a) - ((qt + r) * 128 + b)
                jg = c * OWN - HALO + (qt + r) * 128 + b
                valid = (delta >= 0) & (delta <= 511) & (jg >= 0)
                m[qt, r] = np.where(valid, 0.0, NEG)
        mwin = np.where(a < b, 0.0, NEG).astype(np.float32)
        mcau = np.where(a >= b, 0.0, NEG).astype(np.float32)
        mask_sb = np.concatenate(
            [m.transpose(2, 0, 1, 3).reshape(128, 4 * 640), mwin, mcau],
            axis=1).astype(ml_dtypes.bfloat16)

        in_maps.append(dict(
            xh=xhc, wqt=wqt, wkt=wkt, wvt=wvt, wot=wot, wft=wft, gam=gam,
            cosd=cos_sb, s2d=s2_sb, maskd=mask_sb, identd=ident, onesd=ones,
            tcstd=tcst))
    return in_maps


def kernel(L0_attn_x, L0_attn_Wq, L0_attn_Wk, L0_attn_Wv, L0_attn_Wo,
           L0_ffn_gamma, L0_ffn_W):
    in_maps = _host_inputs(L0_attn_x, L0_attn_Wq, L0_attn_Wk, L0_attn_Wv,
                           L0_attn_Wo, L0_ffn_gamma, L0_ffn_W)
    nc = build_nc(debug=_DEBUG)
    res = run_bass_kernel_spmd(nc, in_maps, core_ids=list(range(NCORES)))
    out = np.concatenate([res.results[c]["o_x"] for c in range(NCORES)], axis=0)
    if _DEBUG:
        kernel._last_results = res
    return out.reshape(1, SEQ, D)
